# version 14
# baseline (speedup 1.0000x reference)
import numpy as np
import ml_dtypes
from contextlib import ExitStack

import concourse.bass as bass
import concourse.mybir as mybir
from concourse.bass_utils import run_bass_kernel_spmd

F32 = mybir.dt.float32
BF16 = mybir.dt.bfloat16
AF = mybir.ActivationFunctionType
ALU = mybir.AluOpType
AX = mybir.AxisListType

S = 2048
D = 2048
I = 8192
BLK = 512
NTS = BLK // 128
NBLK = S // BLK
NISL = I // 512
NK, NV, DK, DV = 16, 32, 128, 128
CHUNK = 64
NCHUNK = S // CHUNK
LN_EPS = 1e-5
CORE_EPS = 1e-6
L2_EPS = 1e-6

ENGS = ("sync", "tensor", "vector", "scalar", "gpsimd")


class Prog:

    def __init__(self, nc):
        self.nc = nc
        self.q = {e: [] for e in ENGS}
        self.sems = {}
        self.cnt = {}
        self.ctx = ExitStack()

    def sem(self, name):
        self.sems[name] = self.ctx.enter_context(self.nc.semaphore(name))
        self.cnt[name] = 0
        return name

    @staticmethod
    def _flat(evs):
        out = []
        for e in evs:
            if e is None:
                continue
            if isinstance(e, tuple):
                out.append(e)
            else:
                out.extend(x for x in e if x is not None)
        best = {}
        for (s, v) in out:
            best[s] = max(best.get(s, 0), v)
        return tuple(best.items())

    def do(self, eng, meth, *a, waits=(), inc=None, incn=1, **kw):
        ev = None
        if inc is not None:
            self.cnt[inc] += incn
            ev = (inc, self.cnt[inc])
        sems = self.sems
        waits = self._flat(waits)

        def fn(E, meth=meth, a=a, kw=kw, waits=waits, inc=inc, incn=incn):
            for i_, (s, v) in enumerate(waits):
                if i_ and i_ % 2 == 0:
                    E.nop()
                E.wait_ge(sems[s], v)
            ins = getattr(E, meth)(*a, **kw)
            if inc is not None:
                ins.then_inc(sems[inc], incn)
        self.q[eng].append(fn)
        return ev

    def wait(self, eng, *evs):
        sems = self.sems
        evs = self._flat(evs)
        if not evs:
            return

        def fn(E, evs=evs):
            for i_, (s, v) in enumerate(evs):
                if i_ and i_ % 2 == 0:
                    E.nop()
                E.wait_ge(sems[s], v)
        self.q[eng].append(fn)

    def dma(self, eng, out, in_, sem, waits=()):
        return self.do(eng, "dma_start", waits=waits, inc=sem, incn=16,
                       out=out, in_=in_)

    def ev(self, sem):
        return (sem, self.cnt[sem])

    def barrier(self, name="bar"):
        if name not in self.sems:
            self.sem(name)
        h = self.sems[name]
        for e in ENGS:
            self.cnt[name] += 1
            self.q[e].append(lambda E, h=h: E.sem_inc(h, 1))
        tgt = self.cnt[name]
        for e in ENGS:
            self.wait(e, (name, tgt))

    def finalize(self, block):
        for e in ENGS:
            ops = self.q[e]

            def runner(E, ops=ops):
                for fn in ops:
                    fn(E)
            getattr(block, e)(runner)


def bf16(x):
    return np.ascontiguousarray(x).astype(ml_dtypes.bfloat16)


FCONST = np.tile(np.array([LN_EPS, CORE_EPS, 1.0, 0.0, 0.0, 0.0, 0.0, 0.0],
                          dtype=np.float32), (128, 1))


def emit_ln(p, xt_ap, sq_ap, mv2, lnt, rstd, nbias, cst, yt_ap,
            sq_waits=(), yt_war=None, x_ev=None):
    pe, ve, ac = "pe", "ve", "ac"
    evq2 = p.do("scalar", "activation", sq_ap, xt_ap, AF.Square,
                waits=[x_ev] + list(sq_waits), inc=ac)
    ev1 = p.do("vector", "tensor_reduce", mv2[:, 0:1], xt_ap, AX.X, ALU.add,
               waits=[x_ev], inc=ve)
    ev2 = p.do("vector", "tensor_reduce", mv2[:, 1:2], sq_ap, AX.X, ALU.add,
               waits=[evq2], inc=ve)
    ev3 = p.do("vector", "tensor_scalar_mul", mv2[:, 0:1], mv2[:, 0:1],
               float(1.0 / D), waits=[ev1], inc=ve)
    ev4 = p.do("vector", "tensor_mul", lnt, mv2[:, 0:1], mv2[:, 0:1],
               waits=[ev3], inc=ve)
    ev5 = p.do("vector", "scalar_tensor_tensor", mv2[:, 1:2], mv2[:, 1:2],
               float(1.0 / D), lnt, op0=ALU.mult, op1=ALU.subtract,
               waits=[ev2, ev4], inc=ve)
    ev6 = p.do("scalar", "activation", lnt, mv2[:, 1:2], AF.Ln,
               bias=cst[:, 0:1], waits=[ev5], inc=ac)
    ev7 = p.do("scalar", "activation", rstd, lnt, AF.Exp, scale=-0.5,
               waits=[ev6], inc=ac)
    ev8 = p.do("vector", "scalar_tensor_tensor", nbias, mv2[:, 0:1], -1.0,
               rstd, op0=ALU.mult, op1=ALU.mult, waits=[ev7, ev3], inc=ve)
    evy = p.do("vector", "scalar_tensor_tensor", yt_ap, xt_ap,
               rstd, nbias.broadcast_to([128, 2048]),
               op0=ALU.mult, op1=ALU.add, waits=[ev8, yt_war], inc=ve)
    return evy


def emit_ffn(p, t, src, dst, wg_h, wu_h, wd_h, tag):
    nc = p.nc
    lx = [p.sem(f"lx{tag}{i}") for i in range(2)]
    wgt = [p.sem(f"wg{tag}{i}") for i in range(2)]
    wut = [p.sem(f"wu{tag}{i}") for i in range(2)]
    wdt = [p.sem(f"wd{tag}{i}") for i in range(2)]
    st = [p.sem(f"st{tag}{i}") for i in range(2)]
    pe, ve, ac, po = "pe", "ve", "ac", "po"

    xt, yt, yfm = t["xt"], t["yt"], t["yfm"]
    wg, wu, wd = t["wg"], t["wu"], t["wd"]
    hscr, habs, tre, cmp_, hs_tm, hst = (t["hscr"], t["habs"], t["tre"],
                                         t["cmp"], t["hs_tm"], t["hst"])
    sscr, res = t["sscr"], t["res"]
    st6, mv2, rstd, nbias, lnt = t["st6"], t["mv2"], t["rstd"], t["nbias"], t["lnt"]
    ident = t["ident"]
    pg, pu, ptr, pf = t["pg"], t["pu"], t["ptr"], t["pf"]

    ev_yt_used = [None, None]
    ev_w_used = [None, None]
    ev_wd_used = [None, None]
    ev_ptr_ev = None
    ev_hst_used = None
    ev_xt_used = [None, None]
    ev_res_used = [None, None]
    ev_sscr_used = [None, None]

    ev_habs_used = None
    ev_hsp_g = None
    for blk in range(NBLK):
        r0 = blk * BLK
        for ts in range(NTS):
            par = ts % 2
            exl = p.dma("sync", xt[par][:], src[r0 + ts * 128: r0 + (ts + 1) * 128, :],
                        lx[par], waits=[e for e in (ev_xt_used[par],) if e])
            evy = emit_ln(p, xt[par][:], hscr[:], mv2, lnt[:], rstd[:],
                          nbias[:], t["cst"], yt[0][:],
                          sq_waits=(ev_hsp_g, ev_habs_used),
                          yt_war=ev_yt_used[0], x_ev=exl)
            ev_xt_used[par] = evy
            p.wait("tensor", evy, ev_ptr_ev)
            for j in range(16):
                p.do("tensor", "transpose",
                     ptr[:, j * 128:(j + 1) * 128],
                     yt[0][:, j * 128:(j + 1) * 128], ident[:],
                     inc=(pe if j == 15 else None))
            ev_yt_used[0] = p.ev(pe)
            ev_ptr_ev = p.do(
                "vector", "tensor_copy",
                yfm[:].rearrange("p (j t) -> p j t", j=16)[:, :, ts * 128:(ts + 1) * 128],
                ptr[:].rearrange("p (j t) -> p j t", j=16),
                waits=[p.ev(pe)], inc=ve)

        ev_hsp = None
        for isl in range(NISL):
            par = isl % 2
            wsl = slice(isl * 512, (isl + 1) * 512)
            ewg = p.dma("gpsimd",
                        wg[par][:].rearrange("p (c i) -> p c i", c=16),
                        wg_h[:, wsl].rearrange("(c p) i -> p c i", p=128),
                        wgt[par], waits=[e for e in (ev_w_used[par],) if e])
            ewu = p.dma("gpsimd",
                        wu[par][:].rearrange("p (c i) -> p c i", c=16),
                        wu_h[:, wsl].rearrange("(c p) i -> p c i", p=128),
                        wut[par])
            ev_g = [None] * NTS
            ev_u = [None] * NTS
            for ts in range(NTS):
                pgx, pux = pg[ts % 2], pu[ts % 2]
                p.wait("tensor", ewg if ts == 0 else None,
                       ewu if ts == 0 else None,
                       ev_ptr_ev if (isl == 0 and ts == 0) else None,
                       t["ev_pg_free"][ts % 2], t["ev_pu_free"][ts % 2],
                       t["ev_pf_free"][ts % 2] if isl == 0 else None,
                       t["ev_pf_free"][2 + ts % 2] if isl == 0 else None)
                t["ev_pg_free"][ts % 2] = None
                t["ev_pu_free"][ts % 2] = None
                if isl == 0:
                    t["ev_pf_free"][ts % 2] = None
                    t["ev_pf_free"][2 + ts % 2] = None
                for c in range(16):
                    lhs = yfm[:, c * 512 + ts * 128: c * 512 + (ts + 1) * 128]
                    ev_g[ts] = p.do("tensor", "matmul", pgx[:], lhs,
                                    wg[par][:, c * 512:(c + 1) * 512],
                                    start=(c == 0), stop=(c == 15),
                                    inc=(pe if c == 15 else None))
                for c in range(16):
                    lhs = yfm[:, c * 512 + ts * 128: c * 512 + (ts + 1) * 128]
                    ev_u[ts] = p.do("tensor", "matmul", pux[:], lhs,
                                    wu[par][:, c * 512:(c + 1) * 512],
                                    start=(c == 0), stop=(c == 15),
                                    inc=(pe if c == 15 else None))
                if ts == NTS - 1:
                    ev_w_used[par] = p.ev(pe)
                evs = p.do("scalar", "activation", sscr[ts % 2][:], pgx[:], AF.Silu,
                           waits=[ev_g[ts]] + [e for e in (ev_sscr_used[ts % 2],) if e],
                           inc=ac)
                t["ev_pg_free"][ts % 2] = evs
                evm = p.do("vector", "tensor_mul", hscr[:, ts * 512:(ts + 1) * 512],
                           sscr[ts % 2][:], pux[:],
                           waits=[evs, ev_u[ts]] + [e for e in (ev_hsp,) if e], inc=ve)
                t["ev_pu_free"][ts % 2] = evm
                ev_sscr_used[ts % 2] = evm
            h4 = hscr[:].rearrange("p (x f) -> p x f", f=4)
            a = [h4[:, :, j] for j in range(4)]
            p.do("vector", "tensor_single_scalar",
                 habs[:].bitcast(mybir.dt.int32), hscr[:].bitcast(mybir.dt.int32),
                 0x7FFFFFFF, ALU.bitwise_and,
                 waits=[e for e in (ev_habs_used,) if e])
            hb4 = habs[:].rearrange("p (x f) -> p x f", f=4)
            b = [hb4[:, :, j] for j in range(4)]
            tr4 = tre[:].rearrange("p (k x) -> p k x", k=4)
            p.do("vector", "tensor_tensor", tr4[:, 0], b[0], b[1], ALU.max)
            p.do("vector", "tensor_tensor", tr4[:, 1], b[2], b[3], ALU.max)
            p.do("vector", "tensor_tensor", tr4[:, 2], b[0], b[1], ALU.min)
            p.do("vector", "tensor_tensor", tr4[:, 3], b[2], b[3], ALU.min)
            p.do("vector", "tensor_tensor", tr4[:, 0], tr4[:, 0], tr4[:, 1], ALU.min)
            p.do("vector", "tensor_tensor", tr4[:, 1], tr4[:, 2], tr4[:, 3], ALU.max)
            evt = p.do("vector", "tensor_tensor", tr4[:, 0], tr4[:, 0], tr4[:, 1],
                       ALU.max, inc=ve)
            cm4 = cmp_[:].rearrange("p (k x) -> p k x", k=4)
            evc = [None] * 4
            for j in range(4):
                evc[j] = p.do("vector", "tensor_tensor", cm4[:, j], b[j], tr4[:, 0],
                              ALU.is_ge, inc=(ve if j == 3 else None))
            ev_habs_used = evc[3]
            hsv = hs_tm[:].rearrange("p (x f) -> p x f", f=4)
            p.wait("gpsimd", ev_hst_used, evc[3])
            for j in range(4):
                ev_hsp = p.do("gpsimd", "tensor_tensor", hsv[:, :, j], a[j],
                              cm4[:, j], ALU.mult,
                              inc=(po if j == 3 else None))
            ev_hsp_g = ev_hsp
            p.wait("tensor", ev_hsp, ev_ptr_ev)
            hsq = hs_tm[:].rearrange("p (ts ic t) -> p ts ic t", ts=NTS, ic=4)
            for ic in range(4):
                for ts in range(NTS):
                    p.do("tensor", "transpose", ptr[:, (ic * 4 + ts) * 128:(ic * 4 + ts + 1) * 128],
                         hsq[:, ts, ic], ident[:],
                         inc=(pe if (ic == 3 and ts == 3) else None))
            ev_hst_used = p.ev(pe)
            ev_ptr_ev = p.do("vector", "tensor_copy",
                             hst[:, (isl * 4) * 512:(isl * 4 + 4) * 512],
                             ptr[:], waits=[p.ev(pe)], inc=ve)

        ev_evac = [None] * NTS
        ev_ich = {}
        for qd in range(4):
            qs = slice(qd * 512, (qd + 1) * 512)
            for ich in range(64):
                par = ich % 2
                wd_war = ev_ich.get(ich - 2)
                if ich < 2 and qd > 0:
                    wd_war = ev_ich.get(62 + ich)
                ewd = p.dma("gpsimd", wd[par][:],
                            wd_h[ich * 128:(ich + 1) * 128, qs],
                            wdt[par], waits=[e for e in (wd_war,) if e])
                p.wait("tensor", ewd,
                       ev_ptr_ev if (qd == 0 and ich == 0) else None)
                if ich == 0 and qd == 0:
                    p.wait("tensor",
                           t["ev_pg_free"][0], t["ev_pg_free"][1],
                           t["ev_pu_free"][0], t["ev_pu_free"][1])
                    t["ev_pg_free"] = [None, None]
                    t["ev_pu_free"] = [None, None]
                if ich == 0 and qd > 0:
                    p.wait("tensor", *[e for e in ev_evac if e])
                for ts in range(NTS):
                    ev = p.do("tensor", "matmul", pf[ts][:],
                              hst[:, (ich * 4 + ts) * 128:(ich * 4 + ts + 1) * 128],
                              wd[par][:],
                              start=(ich == 0), stop=(ich == 63),
                              inc=(pe if ts == NTS - 1 else None))
                ev_ich[ich] = ev
            ev_ich = {63: ev_ich[63]}
            ev_ich[62] = ev_ich[63]
            ev_mm_done = ev_ich[63]
            for ts in range(NTS):
                par = ts % 2
                exl = p.dma("sync", xt[par][:, 0:512],
                            src[r0 + ts * 128: r0 + (ts + 1) * 128, qs],
                            lx[par], waits=[e for e in (ev_xt_used[par],) if e])
                eve = p.do("vector", "scalar_tensor_tensor", res[par][:],
                           pf[ts][:], 0.5, xt[par][:, 0:512],
                           op0=ALU.mult, op1=ALU.add,
                           waits=[exl, ev_mm_done] + [e for e in (ev_res_used[par],) if e],
                           inc=ve)
                ev_evac[ts] = eve
                t["ev_pf_free"][ts] = eve
                ev_xt_used[par] = eve
                ev_res_used[par] = p.dma("sync", dst[r0 + ts * 128: r0 + (ts + 1) * 128, qs],
                                         res[par][:], st[par], waits=[eve])
        ev_xt_used = [ev_xt_used[0], ev_xt_used[1]]
    p.wait("sync", p.ev(st[0]), p.ev(st[1]))
    return [p.ev(st[0]), p.ev(st[1])]


def alloc_ffn_tensors(p, ctx, pre=""):
    nc = p.nc
    t = {}
    t["xt"] = [ctx.enter_context(nc.sbuf_tensor(pre + f"xt{i}", [128, 2048], F32))
               for i in range(2)]
    t["yt"] = [ctx.enter_context(nc.sbuf_tensor(pre + "yt0", [128, 2048], BF16))]
    t["yfm"] = ctx.enter_context(nc.sbuf_tensor(pre + "yfm", [128, 16 * 512], BF16))
    t["wg"] = [ctx.enter_context(nc.sbuf_tensor(pre + f"wg{i}", [128, 16 * 512], BF16))
               for i in range(2)]
    t["wu"] = [ctx.enter_context(nc.sbuf_tensor(pre + f"wu{i}", [128, 16 * 512], BF16))
               for i in range(2)]
    t["wd"] = [ctx.enter_context(nc.sbuf_tensor(pre + f"wd{i}", [128, 512], BF16))
               for i in range(2)]
    t["hscr"] = ctx.enter_context(nc.sbuf_tensor(pre + "hscr", [128, 4 * 512], F32))
    t["habs"] = ctx.enter_context(nc.sbuf_tensor(pre + "habs", [128, 4 * 512], F32))
    t["tre"] = ctx.enter_context(nc.sbuf_tensor(pre + "tre", [128, 4 * 512], F32))
    t["cmp"] = ctx.enter_context(nc.sbuf_tensor(pre + "cmp", [128, 4 * 512], BF16))
    t["hs_tm"] = ctx.enter_context(nc.sbuf_tensor(pre + "hs_tm", [128, 4 * 512], BF16))
    t["hst"] = ctx.enter_context(nc.sbuf_tensor(pre + "hst", [128, 64 * 4 * 128], BF16))
    t["sscr"] = [ctx.enter_context(nc.sbuf_tensor(pre + f"sscr{i}", [128, 512], F32))
                 for i in range(2)]
    t["res"] = [ctx.enter_context(nc.sbuf_tensor(pre + f"res{i}", [128, 512], F32))
                for i in range(2)]
    t["st6"] = ctx.enter_context(nc.sbuf_tensor(pre + "st6", [128, 24], F32))
    t["mv2"] = ctx.enter_context(nc.sbuf_tensor(pre + "mv2", [128, 2], F32))
    t["rstd"] = ctx.enter_context(nc.sbuf_tensor(pre + "rstd", [128, 1], F32))
    t["nbias"] = ctx.enter_context(nc.sbuf_tensor(pre + "nbias", [128, 1], F32))
    t["lnt"] = ctx.enter_context(nc.sbuf_tensor(pre + "lnt", [128, 1], F32))
    ps = [ctx.enter_context(nc.psum_tensor(pre + f"ps{i}", [128, 512], F32))
          for i in range(4)]
    t["pg"] = [ps[0], ps[1]]
    t["pu"] = [ps[2], ps[3]]
    t["pf"] = ps
    t["ptr"] = ctx.enter_context(nc.psum_tensor(pre + "ptr", [128, 16 * 128], BF16))
    t["ev_pg_free"] = [None, None]
    t["ev_pu_free"] = [None, None]
    t["ev_pf_free"] = [None, None, None, None]
    return t


def emit_b1(p, d, tag="b1"):
    nc = p.nc
    pe, ve, ac, po = "pe", "ve", "ac", "po"
    lx = [p.sem(f"{tag}lx{i}") for i in range(2)]
    wq = [p.sem(f"{tag}wq{i}") for i in range(2)]
    stq = p.sem(f"{tag}stq")
    stt_ = [p.sem(f"{tag}stt{i}") for i in range(2)]
    stb = p.sem(f"{tag}stb")
    sty = p.sem(f"{tag}sty")
    ld2 = p.sem(f"{tag}ld2")
    ld3 = p.sem(f"{tag}ld3")
    ldw = p.sem(f"{tag}ldw")
    ctx = d["ctx"]
    ident = d["ident"]
    identf = d["identf"]
    cst = d["cst"]
    keep = d["keep"]

    sb = lambda name, shape, dt_: ctx.enter_context(nc.sbuf_tensor(name, shape, dt_))
    psum = lambda name, shape, dt_: ctx.enter_context(nc.psum_tensor(name, shape, dt_))
    convw = sb("bconvw", [128, 256], F32)
    colsel = sb("bcolsel", [128, 16 * 32], BF16)
    dtb = sb("bdtb", [128, 32], F32)
    nega = sb("bnega", [128, 32], F32)
    gsc1 = sb("bgsc1", [128, 32], F32)
    gsc2 = sb("bgsc2", [128, 32], F32)
    beta_tm = keep["beta_tm"]
    g_tm = sb("g_tm", [128, 16 * 32], F32)
    sst_ = sb("bsst", [64, 2048], F32)
    sstq = sst_[0:32, :]
    sstk = sst_[32:64, :]
    s1ctx = ExitStack()
    s1 = lambda name, shape, dt_: s1ctx.enter_context(
        nc.sbuf_tensor(name, shape, dt_))
    yfm = s1("yfm_b", [128, 16 * 2048], BF16)
    xt = [s1(f"bxt{i}", [128, 2048], F32) for i in range(2)]
    yt = s1("byt", [128, 2048], BF16)
    wqs = [s1(f"wqs{i}", [128, 16 * 128], BF16) for i in range(2)]
    mrow = s1("mrow", [128, 2051], F32)
    acc = s1("bacc", [128, 2048], F32)
    acc2 = s1("bacc2", [128, 2048], F32)
    silf = acc
    qkvt = [s1(f"bqkvt{i}", [128, 2048], BF16) for i in range(2)]
    sqscr = [s1(f"bsq{i}", [128, 2048], BF16) for i in range(2)]
    tmst = [s1(f"btmst{i}", [128, 2048], BF16) for i in range(2)]
    wba = s1("bwba", [128, 16 * 64], BF16)
    st6 = s1("bst6", [128, 24], F32)
    mv2 = s1("bmv2", [128, 2], F32)
    rstd = s1("brstd", [128, 1], F32)
    nbias = s1("bnbias", [128, 1], F32)
    lnt = s1("blnt", [128, 1], F32)

    pm = [psum(f"bpm{i}", [128, 512], F32) for i in range(2)]
    ptr = psum("bptr", [128, 16 * 128], BF16)

    p.dma("sync", convw[:], d["convw"], ld2)
    p.dma("sync", colsel[:], d["colsel"], ld2)
    p.dma("gpsimd", wba[:].rearrange("p (c i) -> p c i", c=16),
          d["wba"].rearrange("(c p) i -> p c i", p=128), ldw)
    p.dma("sync", dtb[:], d["dtb"], ld2)
    p.dma("sync", nega[:], d["alog_rep"], ld2)
    p.dma("sync", keep["S_q"][32:64, :], d["chunksel"], ld2)
    p.dma("sync", keep["R_q"][0:32, :], d["headsel"], ld2)
    eco = p.dma("sync", keep["R_att"][0:32, :], d["headsel"], ld2)
    p.do("gpsimd", "memset", mrow[:, 0:3], 0.0)
    ev_ms = p.do("gpsimd", "memset", mrow[:, 0:3], 0.0, inc=po)
    eea = p.do("scalar", "activation", nega[:], nega[:], AF.Exp, waits=[eco],
               inc=ac)
    ena = p.do("vector", "tensor_scalar_mul", nega[:], nega[:], -1.0, waits=[eea],
               inc=ve)

    ev_xt_used = [None, None]
    ev_yt_used = None
    ev_ptr_ev = None
    for tt in range(16):
        par = tt % 2
        exl = p.dma("sync", xt[par][:], d["x1"][tt * 128:(tt + 1) * 128, :],
                    lx[par], waits=[ev_xt_used[par]])
        evy = emit_ln(p, xt[par][:], acc[:], mv2, lnt[:], rstd[:],
                      nbias[:], cst, yt[:], yt_war=ev_yt_used, x_ev=exl)
        ev_xt_used[par] = evy
        p.wait("tensor", evy, ev_ptr_ev)
        for j in range(16):
            p.do("tensor", "transpose", ptr[:, j * 128:(j + 1) * 128],
                 yt[:, j * 128:(j + 1) * 128], ident[:],
                 inc=(pe if j == 15 else None))
        ev_yt_used = p.ev(pe)
        ev_ptr_ev = p.do(
            "vector", "tensor_copy",
            yfm[:].rearrange("p (j t) -> p j t", j=16)[:, :, tt * 128:(tt + 1) * 128],
            ptr[:].rearrange("p (j t) -> p j t", j=16),
            waits=[p.ev(pe)], inc=ve)
    ey2 = p.dma("sync", d["y2fm"], yfm[:], sty, waits=[ev_ptr_ev])

    ev_pm_free = [None, None]
    for tt in range(16):
        pbg = pm[tt % 2]
        p.wait("tensor", p.ev(ld2) if tt == 0 else None,
               p.ev(ldw) if tt == 0 else None,
               ev_ptr_ev if tt == 0 else None, ev_pm_free[tt % 2])
        ev_pm_free[tt % 2] = None
        for dc in range(16):
            evm = p.do("tensor", "matmul", pbg[:, 0:64],
                       yfm[:, dc * 2048 + tt * 128: dc * 2048 + (tt + 1) * 128],
                       wba[:, dc * 64:(dc + 1) * 64],
                       start=(dc == 0), stop=(dc == 15),
                       inc=(pe if dc == 15 else None))
        p.do("scalar", "activation", beta_tm[:, (2 * tt) * 32:(2 * tt + 1) * 32],
             pbg[0:64, 0:32], AF.Sigmoid, waits=[evm])
        evs = p.do("scalar", "activation",
                   beta_tm[:, (2 * tt + 1) * 32:(2 * tt + 2) * 32],
                   pbg[64:128, 0:32], AF.Sigmoid, inc=ac)
        ev1 = p.do("vector", "tensor_add", gsc1[:], pbg[:, 32:64], dtb[:],
                   waits=[evm, ena], inc=ve)
        evab = p.do("vector", "tensor_single_scalar",
                    gsc2[:].bitcast(mybir.dt.int32), gsc1[:].bitcast(mybir.dt.int32),
                    0x7FFFFFFF, ALU.bitwise_and, waits=[ev1], inc=ve)
        ev2 = p.do("scalar", "activation", gsc2[:], gsc2[:], AF.Exp, scale=-1.0,
                   waits=[evab], inc=ac)
        ev3 = p.do("scalar", "activation", gsc2[:], gsc2[:], AF.Ln,
                   bias=cst[:, 2:3], waits=[ev2], inc=ac)
        ev4 = p.do("vector", "tensor_scalar_max", gsc1[:], gsc1[:], 0.0,
                   waits=[ev3], inc=ve)
        ev5 = p.do("vector", "tensor_add", gsc1[:], gsc1[:], gsc2[:],
                   waits=[ev4], inc=ve)
        evg = p.do("vector", "tensor_mul", g_tm[:, tt * 32:(tt + 1) * 32],
                   gsc1[:], nega[:], waits=[ev5], inc=ve)
        ev_pm_free[tt % 2] = [evg, evs]

    ev_wq_used = [None, None]
    ev_qkv_used = [None, None]
    ev_sq_used = [None, None]
    ev_tm_used = [None, None]
    ev_ss_q = None
    ev_acc_used = None
    pss_ctx = ExitStack()
    pss = pss_ctx.enter_context(nc.psum_tensor("bpss", [32, 2048], F32))
    for cc in range(64):
        par = cc % 2
        ewq = p.dma("gpsimd", wqs[par][:].rearrange("p (c i) -> p c i", c=16),
                    d["wqkv_c"][cc].rearrange("(c p) i -> p c i", p=128),
                    wq[par], waits=[ev_wq_used[par]])
        for tb in range(4):
            pmx = pm[tb % 2]
            p.wait("tensor", ewq if tb == 0 else None, ev_pm_free[tb % 2])
            ev_pm_free[tb % 2] = None
            for dc in range(16):
                evm = p.do("tensor", "matmul", pmx[:],
                           wqs[par][:, dc * 128:(dc + 1) * 128],
                           yfm[:, dc * 2048 + tb * 512: dc * 2048 + (tb + 1) * 512],
                           start=(dc == 0), stop=(dc == 15),
                           inc=(pe if dc == 15 else None))
            if tb == 3:
                ev_wq_used[par] = p.ev(pe)
            eve = p.do("vector", "tensor_copy",
                       mrow[:, 3 + tb * 512: 3 + (tb + 1) * 512], pmx[:],
                       waits=[evm, ev_ms if cc == 0 and tb == 0 else None],
                       inc=ve)
            ev_pm_free[tb % 2] = eve
        w_ = lambda j: convw[:, cc * 4 + j: cc * 4 + j + 1]
        p.do("vector", "tensor_scalar_mul", acc[:], mrow[:, 3:2051], w_(3),
             waits=[ev_acc_used, p.ev(ac) if cc == 0 else None])
        p.do("vector", "scalar_tensor_tensor", acc2[:], mrow[:, 2:2050], w_(2),
             acc[:], op0=ALU.mult, op1=ALU.add)
        p.do("vector", "scalar_tensor_tensor", acc[:], mrow[:, 1:2049], w_(1),
             acc2[:], op0=ALU.mult, op1=ALU.add)
        evcv = p.do("vector", "scalar_tensor_tensor", acc2[:], mrow[:, 0:2048],
                    w_(0), acc[:], op0=ALU.mult, op1=ALU.add, inc=ve)
        is_q = cc < 16
        is_k = 16 <= cc < 32
        if is_q or is_k:
            evsl = p.do("scalar", "activation", silf[:], acc2[:], AF.Silu,
                        waits=[evcv], inc=ac)
            evqk = p.do("vector", "tensor_copy", qkvt[par][:], silf[:],
                        waits=[evsl, ev_qkv_used[par]], inc=ve)
            evsq = p.do("scalar", "activation", sqscr[par][:], silf[:], AF.Square,
                        waits=[evqk, ev_sq_used[par]], inc=ac)
            hh = cc if is_q else cc - 16
            for n in range(4):
                evss = p.do("tensor", "matmul", pss[:, n * 512:(n + 1) * 512],
                            colsel[:, hh * 32:(hh + 1) * 32],
                            sqscr[par][:, n * 512:(n + 1) * 512],
                            start=(hh == 0), stop=(hh == 15),
                            waits=([evsq] if n == 0 else [])
                            + ([ev_ss_q] if (n == 0 and is_k and hh == 0) else []),
                            inc=(pe if n == 3 else None))
            ev_sq_used[par] = evss
            ev_acc_used = evsq
            dst = d["qfm_sp"] if is_q else d["kfm_sp"]
            esp = p.dma("sync", dst[hh], qkvt[par][:], stq, waits=[evqk])
            new_used = [esp]
        else:
            evqk = p.do("scalar", "activation", qkvt[par][:], acc2[:], AF.Silu,
                        waits=[evcv, ev_qkv_used[par]], inc=ac)
            ev_acc_used = evqk
            new_used = []
        if is_k or cc >= 32:
            p.wait("tensor", evqk, ev_ptr_ev)
            for j in range(16):
                p.do("tensor", "transpose", ptr[:, j * 128:(j + 1) * 128],
                     qkvt[par][:, j * 128:(j + 1) * 128], ident[:],
                     inc=(pe if j == 15 else None))
            new_used.append(p.ev(pe))
            ev_ptr_ev = p.do("vector", "tensor_copy", tmst[par][:], ptr[:],
                             waits=[p.ev(pe), ev_tm_used[par]], inc=ve)
            if is_k:
                out_ap = d["ktm_sp"].rearrange("(tt p) c -> p tt c", p=128)[
                    :, :, (cc - 16) * 128:(cc - 15) * 128]
            else:
                out_ap = d["vtm_sp"].rearrange("(tt p) c -> p tt c", p=128)[
                    :, :, (cc - 32) * 128:(cc - 31) * 128]
            ev_tm_used[par] = p.dma("sync", out_ap,
                                    tmst[par][:].rearrange("p (tt c) -> p tt c", tt=16),
                                    stt_[par], waits=[ev_ptr_ev])
        ev_qkv_used[par] = new_used
        if cc == 15:
            ev_ss_q = p.do("vector", "tensor_copy", sstq, pss[:],
                           waits=[p.ev(pe)], inc=ve)
        if cc == 31:
            ev_ss_k = p.do("vector", "tensor_copy", sstk, pss[:],
                           waits=[p.ev(pe)], inc=ve)
    p.wait("tensor", ev_ss_k)
    pss_ctx.close()
    drains = [p.ev(pe), p.ev(ac), p.ev(po), p.ev(ve), p.ev(sty), p.ev(stq),
              p.ev(stt_[0]), p.ev(stt_[1])]
    p.wait("vector", *drains)
    p.wait("scalar", *drains)
    p.wait("tensor", *drains)
    p.wait("gpsimd", *drains)
    s1ctx.close()
    ptf = ctx.enter_context(nc.psum_tensor("bptf", [64, 1024], F32))
    bgfm = sb("bgfm", [64, 2048], F32)
    gc_fm = sb("gc_fm", [32, 2048], F32)
    zeros_ = sb("zeros64", [64, 64], F32)
    zeros64 = zeros_[32:64, :]
    t1t = sb("bt1", [32, 2048], F32)
    t1f = t1t[:, :]
    t2t = sb("bt2", [32, 2048], F32)
    t2f = t2t[:, :]
    t3t = sb("bt3", [32, 2048], F32)
    t3f = t3t[:, :]
    kdlt = sb("bkdl", [32, 2048], F32)
    kdlf = kdlt[:, :]
    ls1t = sb("bls1", [32, 2048], F32)
    ls1 = ls1t[:, :]
    ls2t = sb("bls2", [32, 2048], F32)
    ls2 = ls2t[:, :]
    tmexp = sb("btmexp", [128, 512], F32)
    glc = sb("bglc", [32, 32], F32)
    p.do("gpsimd", "memset", zeros_[:], 0.0, inc=po)

    ev_half = None
    for qd in range(4):
        p.wait("tensor", p.ev(ac) if qd == 0 else None,
               p.ev(ve) if qd == 0 else None, ev_half)
        for cl in range(8):
            ci_ = qd * 8 + cl
            p.do("tensor", "transpose", ptf[0:32, cl * 64:(cl + 1) * 64],
                 beta_tm[:, ci_ * 32:(ci_ + 1) * 32], identf[0:64, 0:64],
                 inc=(pe if cl == 7 else None))
        ev_half = p.do("vector", "tensor_copy",
                       bgfm[0:32, qd * 512:(qd + 1) * 512],
                       ptf[0:32, 0:512], waits=[p.ev(pe)], inc=ve)
    for half in range(2):
        p.wait("tensor", ev_half)
        for tt in range(8 * half, 8 * half + 8):
            off = (tt - 8 * half) * 128
            p.do("tensor", "transpose", ptf[0:32, off:off + 128],
                 g_tm[:, tt * 32:(tt + 1) * 32], identf[:],
                 inc=(pe if tt == 8 * half + 7 else None))
        ev_half = p.do("vector", "tensor_copy",
                       bgfm[32:64, half * 1024:(half + 1) * 1024],
                       ptf[0:32, :], waits=[p.ev(pe)], inc=ve)
    for ci in range(NCHUNK):
        evgc = p.do("vector", "tensor_tensor_scan",
                    gc_fm[:, ci * 64:(ci + 1) * 64],
                    bgfm[32:64, ci * 64:(ci + 1) * 64], zeros64[:],
                    0.0, op0=ALU.add, op1=ALU.add,
                    inc=(ve if ci == NCHUNK - 1 else None))
    evl1 = p.do("scalar", "activation", ls1, sstk, AF.Ln,
                bias=cst[0:32, 1:2], waits=[ev_ss_k], inc=ac)
    evt2 = p.do("vector", "scalar_tensor_tensor", t2f, ls1, -0.5, gc_fm[:, :],
                op0=ALU.mult, op1=ALU.subtract, waits=[evl1, evgc], inc=ve)
    evls2 = p.do("vector", "scalar_tensor_tensor", ls2, ls1, -0.5, gc_fm[:, :],
                 op0=ALU.mult, op1=ALU.add, inc=ve)
    evlb = p.do("scalar", "activation", ls1, bgfm[0:32, :], AF.Ln,
                bias=cst[0:32, 3:4], waits=[evls2], inc=ac)
    evt1 = p.do("vector", "tensor_add", t1f, ls2, ls1, waits=[evlb],
                inc=ve)
    evl1b = p.do("scalar", "activation", ls1, sstk, AF.Ln,
                 bias=cst[0:32, 1:2], waits=[evt1], inc=ac)
    evglc = p.do("vector", "tensor_copy", glc[:],
                 gc_fm[:].rearrange("p (ci t) -> p ci t", ci=32)[:, :, 63],
                 waits=[evgc], inc=ve)
    p.do("vector", "tensor_tensor",
         ls2.rearrange("p (c t) -> p c t", c=32),
         glc[:].rearrange("p c -> p c ()").broadcast_to([32, 32, 64]),
         gc_fm[:].rearrange("p (c t) -> p c t", c=32), ALU.subtract,
         waits=[evglc])
    evkdl = p.do("vector", "scalar_tensor_tensor", kdlf, ls1, -0.5, ls2,
                 op0=ALU.mult, op1=ALU.add, waits=[evl1b], inc=ve)
    evl3 = p.do("scalar", "activation", ls1, sstq, AF.Ln,
                bias=cst[0:32, 1:2], waits=[ev_ss_q, evkdl], inc=ac)
    p.do("vector", "scalar_tensor_tensor", t3f, ls1, -0.5, gc_fm[:, :],
         op0=ALU.mult, op1=ALU.add, waits=[evl3])
    evt3 = p.do("vector", "tensor_scalar_add", t3f, t3f,
                float(-0.5 * np.log(128.0)), inc=ve)
    evsq2 = p.do("vector", "tensor_copy", keep["S_q"][0:32, :], t2f,
                 waits=[evt2], inc=ve)
    eb1 = p.dma("sync", d["tb"][0], t1f, stb, waits=[evt1])
    eb2 = p.dma("sync", d["tb"][1], t2f, stb, waits=[evt2])
    eb3 = p.dma("sync", d["tb"][2], t3f, stb, waits=[evt3])
    er1 = p.dma("sync", keep["R_q"][32:64, :],
                d["tb"][0].rearrange("h (ci f) -> ci h f", ci=32), ld3,
                waits=[eb3])
    er2 = p.dma("sync", keep["R_att"][32:64, :],
                d["tb"][2].rearrange("h (ci f) -> ci h f", ci=32), ld3)
    specs = [(t3f, keep["cq_tm"], evt3, 0), (t1f, keep["bsg_tm"], evt1, 0),
             (kdlf, keep["kdsc_tm"], evkdl, 0)]
    ev_pmt = [None, None]
    for i_, (srcf, dsttm, evin, bp) in enumerate(specs):
        pmx = pm[i_ % 2]
        p.wait("tensor", evin, ev_pmt[i_ % 2], ev_pm_free[i_ % 2])
        ev_pm_free[i_ % 2] = None
        for tt in range(16):
            p.do("tensor", "transpose", pmx[:, tt * 32:(tt + 1) * 32],
                 srcf[:, tt * 128:(tt + 1) * 128],
                 identf[bp:bp + 32, bp:bp + 32],
                 inc=(pe if tt == 15 else None))
        evtc = p.do("vector", "tensor_copy", tmexp[:], pmx[:],
                    waits=[p.ev(pe)], inc=ve)
        dv3 = dsttm[:].rearrange("p (c h) -> p c h", c=32)
        p.do("scalar", "activation", dv3[:, 0::2, :],
             tmexp[0:64, :].rearrange("p (t h) -> p t h", t=16), AF.Exp,
             waits=[evtc])
        ev_pmt[i_ % 2] = p.do("scalar", "activation", dv3[:, 1::2, :],
                              tmexp[64:128, :].rearrange("p (t h) -> p t h", t=16),
                              AF.Exp, inc=ac)
    p.wait("tensor", evglc, p.ev(ve))
    p.do("tensor", "transpose", ptf[0:32, 0:32], glc[:], identf[0:32, 0:32], inc=pe)
    evgt = p.do("vector", "tensor_copy", ls2[:, 0:32], ptf[0:32, 0:32],
                waits=[p.ev(pe)], inc=ve)
    ebg = p.dma("sync", d["glb"], ls2[0:32, 0:32], stb, waits=[evgt])
    erg = p.dma("sync", keep["eglr"][:],
                d["glb"].rearrange("a b -> () (a b)"), ld3, waits=[ebg])
    evex = p.do("scalar", "activation", keep["eglr"][:], keep["eglr"][:], AF.Exp,
                waits=[erg], inc=ac)
    p.do("vector", "tensor_copy", keep["eglrb"][:], keep["eglr"][:],
         waits=[evex], inc=ve)
    p.wait("sync", p.ev(stq), p.ev(stt_[0]), p.ev(stt_[1]), p.ev(sty),
           p.ev(stb), p.ev(ld3))
    return


def emit_b2(p, d, tag="b2"):
    nc = p.nc
    pe, ve, ac, po = "pe", "ve", "ac", "po"
    lq = [p.sem(f"{tag}lq{i}") for i in range(2)]
    lk = [p.sem(f"{tag}lk{i}") for i in range(2)]
    lt = [p.sem(f"{tag}lt{i}") for i in range(2)]
    lv = [p.sem(f"{tag}lv{i}") for i in range(2)]
    stc = p.sem(f"{tag}stc")
    ld4 = p.sem(f"{tag}ld4")
    ctx = d["ctx"]
    ident = d["ident"]
    identf = d["identf"]
    keep = d["keep"]
    sb = lambda name, shape, dt_: ctx.enter_context(nc.sbuf_tensor(name, shape, dt_))
    psum = lambda name, shape, dt_: ctx.enter_context(nc.psum_tensor(name, shape, dt_))

    qf = [sb(f"cqf{i}", [128, 1024], BF16) for i in range(2)]
    kf = [sb(f"ckf{i}", [128, 1024], BF16) for i in range(2)]
    ktm = [sb(f"cktm{i}", [64, 2048], BF16) for i in range(2)]
    vtm = sb("cvtm", [64, 4096], BF16)
    Qw = sb("cQw", [64, 2048], BF16)
    Qtw = sb("cQtw", [64, 2048], BF16)
    Aw = sb("cAw", [64, 2048], BF16)
    Em = sb("cEm", [64, 2048], BF16)
    Emm = sb("cEmm", [64, 2048], BF16)
    Dc = sb("cDc", [64, 1024], F32)
    Pdu = [sb(f"cPd{i}", [64, 2048], BF16) for i in range(4)]
    Pf = [sb(f"cPf{i}", [64, 2048], BF16) for i in range(2)]
    P32 = sb("cP32", [64, 2048], BF16)
    Ech = Pf
    identw = sb("cidw", [64, 2048], BF16)
    onesb = sb("cones", [1, 128], BF16)
    Vbw = sb("cVbw", [64, 4096], BF16)
    Kbgw = sb("cKbgw", [64, 4096], BF16)
    kdww = sb("ckdw", [64, 4096], BF16)
    usbw = sb("cusbw", [64, 4096], BF16)
    wtsw = sb("cwtsw", [128, 2048], BF16)
    wsb = sb("cwsb", [64, 512], F32)
    vns = [sb(f"cvns{i}", [64, 512], F32) for i in range(4)]
    vnb = [sb(f"cvnb{i}", [64, 512], BF16) for i in range(4)]
    osb = [sb(f"cosb{i}", [64, 512], F32) for i in range(2)]
    core_st = sb("ccst", [64, 4096], BF16)
    eglb = sb("ceglb", [128, 32], F32)
    Ssb = sb("cSsb", [128, 4096], F32)
    Sbf = sb("cSbf", [128, 4096], BF16)

    pA = psum("cpA", [64, 1024], F32)
    pB = psum("cpB", [64, 1024], F32)
    pC = psum("cpC", [64, 1024], F32)
    pT16 = psum("cpT", [64, 1024], BF16)
    pK = psum("cpK", [128, 512], F32)

    el1 = p.dma("sync", identw[:], d["identw"], ld4)
    el2 = p.dma("sync", onesb[:], d["ones1b"], ld4)
    p.do("gpsimd", "memset", Ssb[:], 0.0)
    evz = p.do("gpsimd", "memset", Sbf[:], 0.0, inc=po)
    p.wait("tensor", p.ev(ld4))

    free = {"pA": None, "pB": None, "pC": None, "pT": None, "pK": None}

    def use(which, *extra):
        p.wait("tensor", free[which], *extra)
        free[which] = None

    ev_ld_used = {"q": [None, None], "k": [None, None],
                  "t": [None, None], "v": None}
    ev_st_core = None
    ev_eglb_used = None
    ev_wsb_used = None
    ev_kd_used = None
    ev_uw_used = None
    v3 = lambda t_, n: t_[:].rearrange("p (h f) -> p h f", h=n)

    for ci in range(NCHUNK):
        par = ci % 2
        csl = slice(ci * 64, (ci + 1) * 64)
        eq = p.dma("sync", qf[par][:].rearrange("p (h t) -> p h t", h=16),
                   d["qfm_sp"].rearrange("h p t -> p h t")[:, :, csl],
                   lq[par], waits=[ev_ld_used["q"][par]])
        ek = p.dma("sync", kf[par][:].rearrange("p (h t) -> p h t", h=16),
                   d["kfm_sp"].rearrange("h p t -> p h t")[:, :, csl],
                   lk[par], waits=[ev_ld_used["k"][par]])
        et = p.dma("sync", ktm[par][:], d["ktm_sp"][csl, :], lt[par],
                   waits=[ev_ld_used["t"][par]])
        ev_ = p.dma("sync", vtm[:], d["vtm_sp"][csl, :], lv[0],
                    waits=[ev_ld_used["v"]])
        use("pK")
        p.do("tensor", "matmul", pK[:, 0:32], onesb[:],
             keep["eglrb"][:, ci * 32:(ci + 1) * 32], start=True, stop=True,
             inc=pe)
        ev_eglb = p.do("vector", "tensor_copy", eglb[:], pK[:, 0:32],
                       waits=[(p.ev(pe))] + ([ev_eglb_used] if ev_eglb_used else []),
                       inc=ve)
        free["pK"] = ev_eglb
        use("pA", ek, eq)
        for kh in range(16):
            p.do("tensor", "matmul", pA[:, kh * 64:(kh + 1) * 64],
                 kf[par][:, kh * 64:(kh + 1) * 64],
                 kf[par][:, kh * 64:(kh + 1) * 64], start=True, stop=True,
                 inc=(pe if kh == 15 else None))
        use("pB")
        for kh in range(16):
            p.do("tensor", "matmul", pB[:, kh * 64:(kh + 1) * 64],
                 kf[par][:, kh * 64:(kh + 1) * 64],
                 qf[par][:, kh * 64:(kh + 1) * 64], start=True, stop=True,
                 inc=(pe if kh == 15 else None))
        ev_ggq = p.ev(pe)
        ev_clamp = [None, None]
        for half in range(2):
            use("pC", p.ev(ve) if (ci == 0 and half == 0) else None)
            for n in range(2):
                evd = p.do("tensor", "matmul", pC[:, n * 512:(n + 1) * 512],
                           keep["S_q"][:, csl],
                           keep["R_q"][:, half * 1024 + n * 512:
                                       half * 1024 + (n + 1) * 512],
                           start=True, stop=True,
                           inc=(pe if n == 1 else None))
            ev_clamp[half] = p.do(
                "vector", "tensor_single_scalar", Dc[:], pC[:], 30.0, ALU.min,
                waits=[evd], inc=ve)
            free["pC"] = ev_clamp[half]
            p.do("scalar", "activation", Em[:, half * 1024:(half + 1) * 1024],
                 Dc[:], AF.Exp, waits=[ev_clamp[half]],
                 inc=(ac if half == 1 else None))
        evex = p.ev(ac)
        evmk = p.do("gpsimd", "affine_select", Emm[:], Em[:],
                    [[0, 32], [1, 64]], ALU.is_gt, 0.0,
                    base=0, channel_multiplier=-1, waits=[evex], inc=po)
        qv = Qw[:].rearrange("p (h f) -> p h f", h=32)
        emq3 = Emm[:].rearrange("p (h f) -> p h f", h=32)
        g3 = pA[:].rearrange("p (k f) -> p k f", k=16)
        p.do("vector", "scalar_tensor_tensor", qv[:, 0::2], g3, -1.0,
             emq3[:, 0::2], op0=ALU.mult, op1=ALU.mult,
             waits=[evmk, ev_ggq])
        evq = p.do("vector", "scalar_tensor_tensor", qv[:, 1::2], g3, -1.0,
                   emq3[:, 1::2], op0=ALU.mult, op1=ALU.mult, inc=ve)
        ev_emt = None
        for half in range(2):
            use("pT", evmk if half == 0 else None)
            for hh in range(16):
                h = half * 16 + hh
                p.do("tensor", "transpose", pT16[:, hh * 64:(hh + 1) * 64],
                     Emm[:, h * 64:(h + 1) * 64], ident[0:64, 0:64],
                     inc=(pe if hh == 15 else None))
            ev_emt = p.do("vector", "tensor_copy",
                          Emm[:, half * 1024:(half + 1) * 1024], pT16[:],
                          waits=[(p.ev(pe)), evq], inc=ve)
            free["pT"] = ev_emt
        qtv = Qtw[:].rearrange("p (h f) -> p h f", h=32)
        et_ = Emm[:].rearrange("p (h f) -> p h f", h=32)
        p.do("vector", "scalar_tensor_tensor", qtv[:, 0::2], g3, -1.0,
             et_[:, 0::2], op0=ALU.mult, op1=ALU.mult, waits=[ev_emt])
        evqt = p.do("vector", "scalar_tensor_tensor", qtv[:, 1::2], g3, -1.0,
                    et_[:, 1::2], op0=ALU.mult, op1=ALU.mult, inc=ve)
        free["pA"] = evqt
        ev_clamp = [None, None]
        for half in range(2):
            use("pC")
            for n in range(2):
                evd = p.do("tensor", "matmul", pC[:, n * 512:(n + 1) * 512],
                           keep["S_q"][:, csl],
                           keep["R_att"][:, half * 1024 + n * 512:
                                         half * 1024 + (n + 1) * 512],
                           start=True, stop=True,
                           inc=(pe if n == 1 else None))
            ev_clamp[half] = p.do(
                "vector", "tensor_single_scalar", Dc[:], pC[:], 30.0, ALU.min,
                waits=[evd, evex if half == 0 else None,
                       evqt if half == 0 else None], inc=ve)
            free["pC"] = ev_clamp[half]
            p.do("scalar", "activation", Em[:, half * 1024:(half + 1) * 1024],
                 Dc[:], AF.Exp, waits=[ev_clamp[half]],
                 inc=(ac if half == 1 else None))
        evex2 = p.ev(ac)
        evmk2 = p.do("gpsimd", "affine_select", Emm[:], Em[:],
                     [[0, 32], [1, 64]], ALU.is_ge, 0.0,
                     base=0, channel_multiplier=-1,
                     waits=[evex2, evqt], inc=po)
        av = Aw[:].rearrange("p (h f) -> p h f", h=32)
        em3 = Emm[:].rearrange("p (h f) -> p h f", h=32)
        gq3 = pB[:].rearrange("p (k f) -> p k f", k=16)
        p.do("vector", "tensor_tensor", av[:, 0::2], gq3, em3[:, 0::2],
             ALU.mult, waits=[evmk2])
        evaw = p.do("vector", "tensor_tensor", av[:, 1::2], gq3, em3[:, 1::2],
                    ALU.mult, inc=ve)
        free["pB"] = evaw
        cur_t, cur_f = Qtw, Qw
        for lvl in range(5):
            dst_f = P32 if lvl == 4 else Pf[lvl % 2]
            for half in range(2):
                use("pC", (evqt if lvl == 0 and half == 0 else None))
                for hh in range(16):
                    h = half * 16 + hh
                    p.do("tensor", "matmul", pC[:, hh * 64:(hh + 1) * 64],
                         cur_t[:, h * 64:(h + 1) * 64],
                         cur_f[:, h * 64:(h + 1) * 64], start=True, stop=True,
                         inc=(pe if hh == 15 else None))
                evf = p.do("vector", "tensor_copy",
                           dst_f[:, half * 1024:(half + 1) * 1024], pC[:],
                           waits=[p.ev(pe)], inc=ve)
                free["pC"] = evf
                if lvl < 4:
                    use("pA")
                    for hh in range(16):
                        h = half * 16 + hh
                        p.do("tensor", "matmul", pA[:, hh * 64:(hh + 1) * 64],
                             cur_f[:, h * 64:(h + 1) * 64],
                             cur_t[:, h * 64:(h + 1) * 64], start=True,
                             stop=True, inc=(pe if hh == 15 else None))
                    evd_ = p.do("vector", "tensor_copy",
                                Pdu[lvl][:, half * 1024:(half + 1) * 1024],
                                pA[:], waits=[p.ev(pe)], inc=ve)
                    free["pA"] = evd_
            if lvl < 4:
                cur_t, cur_f = Pdu[lvl], dst_f
        ev_pw = p.ev(ve)
        evch = p.do("vector", "tensor_add", Ech[0][:], P32[:], identw[:],
                    waits=[ev_pw], inc=ve)
        chain = [Pdu[3], Pdu[2], Pdu[1], Pdu[0], Qtw]
        cur = 0
        for Pt_ in chain:
            nxt = 1 - cur
            for half in range(2):
                use("pB", evch if half == 0 else None)
                for hh in range(16):
                    h = half * 16 + hh
                    p.do("tensor", "matmul", pB[:, hh * 64:(hh + 1) * 64],
                         Pt_[:, h * 64:(h + 1) * 64],
                         Ech[cur][:, h * 64:(h + 1) * 64], start=True,
                         stop=True, inc=(pe if hh == 15 else None))
                evch = p.do("vector", "tensor_add",
                            Ech[nxt][:, half * 1024:(half + 1) * 1024], pB[:],
                            Ech[cur][:, half * 1024:(half + 1) * 1024],
                            waits=[p.ev(pe)], inc=ve)
                free["pB"] = evch
            cur = nxt
        Fw = Ech[cur]
        ev_F = evch
        def sclw(t_, u_):
            return t_[:, ci * 32 + u_:ci * 32 + 32:2].rearrange(
                "p h -> p h ()").broadcast_to([64, 16, 128])

        kt3 = ktm[par][:].rearrange("p (k dd) -> p k dd", k=16)
        evvb1 = p.do("vector", "tensor_tensor",
                     v3(Vbw, 32)[:, 0::2], v3(vtm, 32)[:, 0::2],
                     sclw(keep["beta_tm"], 0), ALU.mult,
                     waits=[ev_, ev_uw_used], inc=ve)
        evvb = p.do("vector", "tensor_tensor",
                    v3(Vbw, 32)[:, 1::2], v3(vtm, 32)[:, 1::2],
                    sclw(keep["beta_tm"], 1), ALU.mult, inc=ve)
        p.do("vector", "tensor_tensor", v3(Kbgw, 32)[:, 0::2], kt3,
             sclw(keep["bsg_tm"], 0), ALU.mult, waits=[et])
        evkb = p.do("vector", "tensor_tensor", v3(Kbgw, 32)[:, 1::2], kt3,
                    sclw(keep["bsg_tm"], 1), ALU.mult, inc=ve)
        p.do("vector", "tensor_tensor", v3(kdww, 32)[:, 0::2], kt3,
             sclw(keep["kdsc_tm"], 0), ALU.mult, waits=[ev_kd_used])
        evkd = p.do("vector", "tensor_tensor", v3(kdww, 32)[:, 1::2], kt3,
                    sclw(keep["kdsc_tm"], 1), ALU.mult, inc=ve)
        for grp in range(4):
            use("pA", ev_F if grp == 0 else None,
                evvb if grp == 0 else None)
            for j in range(8):
                h = grp * 8 + j
                p.do("tensor", "matmul", pA[:, j * 128:(j + 1) * 128],
                     Fw[:, h * 64:(h + 1) * 64],
                     Vbw[:, h * 128:(h + 1) * 128], start=True, stop=True,
                     inc=(pe if j == 7 else None))
            evu = p.do("vector", "tensor_copy",
                       usbw[:, grp * 1024:(grp + 1) * 1024], pA[:],
                       waits=[p.ev(pe)], inc=ve)
            free["pA"] = evu
            use("pC", evkb if grp == 0 else None)
            for j in range(8):
                h = grp * 8 + j
                p.do("tensor", "matmul", pC[:, j * 128:(j + 1) * 128],
                     Fw[:, h * 64:(h + 1) * 64],
                     Kbgw[:, h * 128:(h + 1) * 128], start=True, stop=True,
                     inc=(pe if j == 7 else None))
            evwc = p.do("vector", "tensor_copy", wsb[:],
                        pC[:, 0:512], waits=[p.ev(pe), ev_wsb_used], inc=ve)
            use("pK")
            for j in range(4):
                p.do("tensor", "transpose", pK[:, j * 64:(j + 1) * 64],
                     wsb[:, j * 128:(j + 1) * 128], identf[0:64, 0:64],
                     waits=[evwc] if j == 0 else (),
                     inc=(pe if j == 3 else None))
            evt1_ = p.do("vector", "tensor_copy",
                         wtsw[:, grp * 512:grp * 512 + 256], pK[:, 0:256],
                         waits=[p.ev(pe)], inc=ve)
            evwc2 = p.do("vector", "tensor_copy", wsb[:],
                         pC[:, 512:1024], waits=[evt1_], inc=ve)
            free["pC"] = evwc2
            use("pK")
            for j in range(4):
                p.do("tensor", "transpose", pK[:, j * 64:(j + 1) * 64],
                     wsb[:, j * 128:(j + 1) * 128], identf[0:64, 0:64],
                     waits=[evwc2] if j == 0 else (),
                     inc=(pe if j == 3 else None))
            ev_wsb_used = p.do("vector", "tensor_copy",
                               wtsw[:, grp * 512 + 256:(grp + 1) * 512],
                               pK[:, 0:256], waits=[p.ev(pe)], inc=ve)
            free["pK"] = ev_wsb_used
        ev_uw_used = p.ev(ve)
        slots = [(pA, 0), (pA, 512), (pC, 0), (pC, 512)]
        slot_free = [free["pA"], free["pA"], free["pC"], free["pC"]]
        free["pA"] = None
        free["pC"] = None
        ev_vnb = [None] * 8
        ev_kdone = [None] * 8
        ev_glue_prev = d.setdefault("_b2_glue", [None] * 8)
        ev_glue = [None] * 8

        def stage_A(s_):
            ps, off = slots[s_ % 4]
            p.wait("tensor", slot_free[s_ % 4], ev_glue_prev[s_],
                   evz if (ci == 0 and s_ == 0) else None)
            slot_free[s_ % 4] = None
            for j in range(4):
                h = s_ * 4 + j
                p.do("tensor", "matmul", ps[:, off + j * 128: off + (j + 1) * 128],
                     wtsw[:, h * 64:(h + 1) * 64],
                     Sbf[:, h * 128:(h + 1) * 128], start=True, stop=True,
                     inc=(pe if j == 3 else None))
            return p.ev(pe)

        def stage_B(s_, evA):
            ps, off = slots[s_ % 4]
            sp4 = s_ % 4
            evvs = p.do("vector", "tensor_sub", vns[sp4][:],
                        usbw[:, s_ * 512:(s_ + 1) * 512],
                        ps[:, off:off + 512], waits=[evA], inc=ve)
            slot_free[s_ % 4] = evvs
            ev_vnb[s_] = p.do("scalar", "activation", vnb[sp4][:],
                              vns[sp4][:], AF.Copy, waits=[evvs], inc=ac)

        def stage_C(s_):
            h0 = s_ * 4
            use("pB", ev_vnb[s_])
            for j in range(4):
                h = h0 + j
                p.do("tensor", "matmul", pB[:, j * 128:(j + 1) * 128],
                     qf[par][:, (h // 2) * 64:(h // 2 + 1) * 64],
                     Sbf[:, h * 128:(h + 1) * 128], start=True, stop=True)
            for j in range(4):
                h = h0 + j
                p.do("tensor", "matmul",
                     pB[:, 512 + j * 128: 512 + (j + 1) * 128],
                     Aw[:, h * 64:(h + 1) * 64],
                     vnb[s_ % 4][:, j * 128:(j + 1) * 128], start=True,
                     stop=True, inc=(pe if j == 3 else None))
            evqa = p.ev(pe)
            use("pK", evkd if s_ == 0 else None)
            for j in range(4):
                h = h0 + j
                p.do("tensor", "matmul", pK[:, j * 128:(j + 1) * 128],
                     kdww[:, h * 128:(h + 1) * 128],
                     vnb[s_ % 4][:, j * 128:(j + 1) * 128], start=True,
                     stop=True, inc=(pe if j == 3 else None))
            ev_kdone[s_] = p.ev(pe)
            return evqa

        def stage_D(s_, evqa):
            h0 = s_ * 4
            sp2 = s_ % 2

            def sclq():
                return keep["cq_tm"][:, ci * 32 + h0:ci * 32 + h0 + 4].rearrange(
                    "p h -> p h ()").broadcast_to([64, 4, 128])

            evo1 = p.do("vector", "tensor_tensor", v3(osb[sp2], 4),
                        pB[:, 0:512].rearrange("p (h f) -> p h f", h=4),
                        sclq(), ALU.mult,
                        waits=[evqa, ev_glue[s_ - 2] if s_ >= 2 else None],
                        inc=ve)
            evo2 = p.do("vector", "tensor_add",
                        core_st[:, h0 * 128:(h0 + 4) * 128], osb[sp2][:],
                        pB[:, 512:1024],
                        waits=[evo1] + ([ev_st_core] if (s_ == 0 and ev_st_core)
                                        else []), inc=ve)
            free["pB"] = evo2
            egl3 = eglb[:, h0:h0 + 4].rearrange("p h -> p h ()").broadcast_to(
                [128, 4, 128])
            S3 = Ssb[:, h0 * 128:(h0 + 4) * 128].rearrange(
                "p (h dd) -> p h dd", h=4)
            p.do("vector", "tensor_tensor", S3, S3, egl3, ALU.mult,
                 waits=[ev_eglb, ev_kdone[s_]])
            evs2 = p.do("vector", "tensor_add",
                        Ssb[:, h0 * 128:(h0 + 4) * 128],
                        Ssb[:, h0 * 128:(h0 + 4) * 128], pK[:], inc=ve)
            free["pK"] = evs2
            evs3 = p.do("vector", "tensor_copy",
                        Sbf[:, h0 * 128:(h0 + 4) * 128],
                        Ssb[:, h0 * 128:(h0 + 4) * 128], inc=ve)
            ev_glue[s_] = evs3
            ev_glue_prev[s_] = evs3
            return evs3

        evA = {}
        evC = {}
        for s_ in range(8):
            evA[s_] = stage_A(s_)
            if s_ >= 1:
                stage_B(s_ - 1, evA[s_ - 1])
            if s_ >= 2:
                evC[s_ - 2] = stage_C(s_ - 2)
                stage_D(s_ - 2, evC[s_ - 2])
        stage_B(7, evA[7])
        evC[6] = stage_C(6)
        stage_D(6, evC[6])
        evC[7] = stage_C(7)
        stage_D(7, evC[7])
        free["pA"] = [e for e in (slot_free[0], slot_free[1]) if e] or None
        free["pC"] = [e for e in (slot_free[2], slot_free[3]) if e] or None
        ev_eglb_used = p.ev(ve)
        ev_kd_used = p.ev(pe)
        ev_st_core = p.dma("sync", d["core_sp"][csl, :], core_st[:], stc,
                           waits=[p.ev(ve)])
        ev_ld_used["q"][par] = p.ev(pe)
        ev_ld_used["k"][par] = p.ev(pe)
        ev_ld_used["t"][par] = p.ev(ve)
        ev_ld_used["v"] = ev_uw_used
    p.wait("sync", p.ev(stc))
    return


def w4b(t_):
    return t_[:].rearrange("p (k u dd) -> p k u dd", k=2, u=2)


def v3b(ps, n):
    return ps[:, 0:512].rearrange("p (h f) -> p h f", h=n)


def emit_c(p, d, tag="c"):
    nc = p.nc
    pe, ve, ac, po = "pe", "ve", "ac", "po"
    ld5 = p.sem(f"{tag}ld5")
    wz = [p.sem(f"{tag}wz{i}") for i in range(2)]
    stz = [p.sem(f"{tag}stz{i}") for i in range(2)]
    lcr = p.sem(f"{tag}lcr")
    lsz = [p.sem(f"{tag}lsz{i}") for i in range(2)]
    lx1 = [p.sem(f"{tag}lx1{i}") for i in range(2)]
    stx = [p.sem(f"{tag}stx{i}") for i in range(2)]
    wo = p.sem(f"{tag}wo")
    ctx = d["ctx"]
    ident = d["ident"]
    cst = d["cst"]
    sb = lambda name, shape, dt_: ctx.enter_context(nc.sbuf_tensor(name, shape, dt_))
    psum = lambda name, shape, dt_: ctx.enter_context(nc.psum_tensor(name, shape, dt_))

    p1 = ExitStack()
    yfm = p1.enter_context(nc.sbuf_tensor("zyfm", [128, 16 * 2048], BF16))
    wzb = [p1.enter_context(nc.sbuf_tensor(f"zwz{i}", [128, 16 * 512], BF16))
           for i in range(2)]
    szt = [p1.enter_context(nc.sbuf_tensor(f"zszt{i}", [128, 512], BF16))
           for i in range(2)]
    pz = [p1.enter_context(nc.psum_tensor(f"zpz{i}", [128, 512], F32))
          for i in range(2)]
    ey = p.dma("sync", yfm[:], d["y2fm"], ld5)
    ev_wz_used = [None, None]
    ev_sz_used = [None, None]
    ev_pz_free = [None, None]
    for zs in range(8):
        par = zs % 2
        ewz = p.dma("gpsimd", wzb[par][:].rearrange("p (c i) -> p c i", c=16),
                    d["wz"][:, zs * 512:(zs + 1) * 512].rearrange(
                        "(c p) i -> p c i", p=128),
                    wz[par], waits=[ev_wz_used[par]])
        for tt in range(16):
            pzx = pz[tt % 2]
            p.wait("tensor", ewz if tt == 0 else None,
                   ey if (zs == 0 and tt == 0) else None, ev_pz_free[tt % 2])
            ev_pz_free[tt % 2] = None
            for dc in range(16):
                evm = p.do("tensor", "matmul", pzx[:],
                           yfm[:, dc * 2048 + tt * 128: dc * 2048 + (tt + 1) * 128],
                           wzb[par][:, dc * 512:(dc + 1) * 512],
                           start=(dc == 0), stop=(dc == 15),
                           inc=(pe if dc == 15 else None))
            if tt == 15:
                ev_wz_used[par] = p.ev(pe)
            evs = p.do("scalar", "activation", szt[tt % 2][:], pzx[:], AF.Silu,
                       waits=[evm, ev_sz_used[tt % 2]], inc=ac)
            ev_pz_free[tt % 2] = evs
            ev_sz_used[tt % 2] = p.dma(
                "sync", d["sz_sp"][tt * 128:(tt + 1) * 128, zs * 512:(zs + 1) * 512],
                szt[tt % 2][:], stz[tt % 2], waits=[evs])
    p.wait("sync", p.ev(stz[0]), p.ev(stz[1]))
    p.barrier(f"{tag}bar1")
    p1.close()

    wob = sb("cwo", [128, 32 * 2048], BF16)
    crt = sb("ccrt", [128, 4096], BF16)
    szr = [sb(f"cszr{i}", [128, 4096], BF16) for i in range(2)]
    x1t = [sb(f"cx1t{i}", [128, 512], F32) for i in range(2)]
    res = [sb(f"cres{i}", [128, 512], F32) for i in range(2)]
    sq = sb("csq", [128, 4096], BF16)
    rms = sb("crms", [128, 32], F32)
    rstd = sb("crstd", [128, 32], F32)
    ctb = sb("cctb", [128, 4096], BF16)
    cfm = sb("ccfm", [128, 4096], BF16)
    normw = sb("cnw", [128, 1], F32)
    ptr = psum("cptr", [128, 2048], BF16)
    pO = [psum(f"cpO{i}", [128, 512], F32) for i in range(2)]

    env = p.dma("sync", normw[:], d["normw"], ld5)
    ev_wo_ld = []
    for dv in range(32):
        e_ = p.dma("gpsimd", wob[:, dv * 2048:(dv + 1) * 2048],
                   d["wout"][dv * 128:(dv + 1) * 128, :], wo)
        ev_wo_ld.append(e_)
    evnw = []
    for dv in range(32):
        evnw.append(p.do("vector", "tensor_scalar_mul",
                         wob[:, dv * 2048:(dv + 1) * 2048],
                         wob[:, dv * 2048:(dv + 1) * 2048], normw[:],
                         waits=[ev_wo_ld[dv], env], inc=ve))
    ev_crt_used = None
    ev_szr_used = [None, None]
    ev_ptr_ev = None
    ev_pO_free = [None, None]
    ev_x1_used = [None, None]
    ev_res_used = [None, None]
    for tt in range(16):
        par = tt % 2
        ecr = p.dma("sync", crt[:], d["core_sp"][tt * 128:(tt + 1) * 128, :],
                    lcr, waits=[ev_crt_used])
        esz = p.dma("sync", szr[par][:],
                    d["sz_sp"][tt * 128:(tt + 1) * 128, :], lsz[par],
                    waits=[ev_szr_used[par]])
        evsq = p.do("scalar", "activation", sq[:], crt[:], AF.Square,
                    waits=[ecr], inc=ac)
        evrm = p.do("vector", "tensor_reduce",
                    rms[:].rearrange("p h -> p h ()"),
                    sq[:].rearrange("p (h dd) -> p h dd", h=32),
                    AX.X, ALU.add, waits=[evsq], inc=ve)
        evln = p.do("scalar", "activation", rstd[:], rms[:], AF.Ln,
                    scale=float(1.0 / 128.0), bias=cst[:, 1:2], waits=[evrm],
                    inc=ac)
        evrs = p.do("scalar", "activation", rstd[:], rstd[:], AF.Exp, scale=-0.5,
                    waits=[evln], inc=ac)
        evg1 = p.do("vector", "tensor_tensor",
                    ctb[:].rearrange("p (h dd) -> p h dd", h=32),
                    crt[:].rearrange("p (h dd) -> p h dd", h=32),
                    rstd[:].rearrange("p h -> p h ()").broadcast_to([128, 32, 128]),
                    ALU.mult, waits=[evrs], inc=ve)
        ev_crt_used = evg1
        evg2 = p.do("vector", "tensor_mul", ctb[:], ctb[:], szr[par][:],
                    waits=[esz], inc=ve)
        ev_szr_used[par] = evg2
        evcf = None
        for rnd in range(2):
            p.wait("tensor", evg2 if rnd == 0 else None, ev_ptr_ev)
            for j in range(16):
                p.do("tensor", "transpose", ptr[:, j * 128:(j + 1) * 128],
                     ctb[:, rnd * 2048 + j * 128: rnd * 2048 + (j + 1) * 128],
                     ident[:], inc=(pe if j == 15 else None))
            ev_ptr_ev = p.do("vector", "tensor_copy",
                             cfm[:, rnd * 2048:(rnd + 1) * 2048], ptr[:],
                             waits=[p.ev(pe)], inc=ve)
            evcf = ev_ptr_ev
        for ds in range(4):
            pox = pO[ds % 2]
            p.wait("tensor", evcf if ds == 0 else None,
                   evnw[-1] if tt == 0 and ds == 0 else None,
                   ev_pO_free[ds % 2])
            ev_pO_free[ds % 2] = None
            for dv in range(32):
                evm = p.do("tensor", "matmul", pox[:],
                           cfm[:, dv * 128:(dv + 1) * 128],
                           wob[:, dv * 2048 + ds * 512: dv * 2048 + (ds + 1) * 512],
                           start=(dv == 0), stop=(dv == 31),
                           inc=(pe if dv == 31 else None))
            exl = p.dma("sync", x1t[ds % 2][:],
                        d["x1"][tt * 128:(tt + 1) * 128, ds * 512:(ds + 1) * 512],
                        lx1[ds % 2], waits=[ev_x1_used[ds % 2]])
            evr = p.do("vector", "tensor_add", res[ds % 2][:], pox[:],
                       x1t[ds % 2][:],
                       waits=[evm, exl, ev_res_used[ds % 2]], inc=ve)
            ev_pO_free[ds % 2] = evr
            ev_x1_used[ds % 2] = evr
            ev_res_used[ds % 2] = p.dma(
                "sync", d["x2"][tt * 128:(tt + 1) * 128, ds * 512:(ds + 1) * 512],
                res[ds % 2][:], stx[ds % 2], waits=[evr])
    p.wait("sync", p.ev(stx[0]), p.ev(stx[1]))
    return


ALL_STAGES = ("A", "B1", "B2", "C", "D")


def build_program(stages=ALL_STAGES, dbg=()):
    nc = bass.Bass("TRN2", target_bir_lowering=False)
    dram = lambda name, shape, dt_, kind: nc.dram_tensor(name, shape, dt_,
                                                         kind=kind).ap()
    ei = lambda name, shape, dt_: dram(name, shape, dt_, "ExternalInput")

    def scratch(name, shape, dt_):
        kind = "ExternalOutput" if name in dbg else "Internal"
        return dram(name, shape, dt_, kind)

    hs = ei("hs", [S, D], F32)
    ident_h = ei("c_ident", [128, 128], BF16)
    fconst_h = ei("c_fconst", [128, 8], F32)
    out = dram("out", [S, D], F32, "ExternalOutput")
    need_a = "A" in stages
    need_b = "B1" in stages or "B2" in stages
    need_c = "C" in stages
    need_d = "D" in stages
    if need_a:
        wg_h = ei("ffn_wg", [D, I], F32)
        wu_h = ei("ffn_wu", [D, I], F32)
        wd_h = ei("ffn_wd", [I, D], F32)
    elif need_d:
        wg_h = ei("ffn_wg", [D, I], F32)
        wu_h = ei("ffn_wu", [D, I], F32)
        wd_h = ei("ffn_wd", [I, D], F32)
    x1 = scratch("x1", [S, D], F32) if need_a else (
        ei("x1", [S, D], F32) if (need_b or need_c) else None)
    x2 = scratch("x2", [S, D], F32) if need_c else (
        ei("x2", [S, D], F32) if need_d else None)

    d = {}
    if need_b or need_c:
        d["y2fm"] = (scratch("y2fm", [128, 16 * 2048], BF16) if need_b
                     else ei("y2fm", [128, 16 * 2048], BF16))
    if need_b:
        d["wqkv_c"] = ei("wqkv_c", [64, D, 128], F32)
        d["wba"] = ei("wba", [D, 64], F32)
        d["convw"] = ei("convw", [128, 256], F32)
        d["dtb"] = ei("dtb", [128, 32], F32)
        d["alog_rep"] = ei("alog_rep", [128, 32], F32)
        d["colsel"] = ei("colsel", [128, 16 * 32], BF16)
        d["headsel"] = ei("headsel", [32, 2048], F32)
        d["chunksel"] = ei("chunksel", [32, 2048], F32)
        d["identw"] = ei("identw", [64, 2048], BF16)
        d["ones1b"] = ei("ones1b", [1, 128], BF16)
        d["qfm_sp"] = scratch("qfm_sp", [16, 128, S], BF16)
        d["kfm_sp"] = scratch("kfm_sp", [16, 128, S], BF16)
        d["ktm_sp"] = scratch("ktm_sp", [S, 2048], BF16)
        d["vtm_sp"] = scratch("vtm_sp", [S, 4096], BF16)
        d["tb"] = [scratch(f"tb{i}", [32, 2048], F32) for i in range(3)]
        d["glb"] = scratch("glb", [32, 32], F32)
    if need_b or need_c:
        d["core_sp"] = (scratch("core_sp", [S, 4096], BF16) if need_b
                        else ei("core_sp", [S, 4096], BF16))
    if need_c:
        d["wz"] = ei("wz", [D, 4096], F32)
        d["wout"] = ei("wout", [4096, D], F32)
        d["normw"] = ei("normw", [128, 1], F32)
        d["sz_sp"] = scratch("sz_sp", [S, 4096], BF16)
    d["x1"] = x1
    d["x2"] = x2

    p = Prog(nc)
    with p.ctx:
        for s_ in ("pe", "ve", "ac", "po", "ld0", "ld1", "ldg"):
            p.sem(s_)
        gctx = p.ctx.enter_context(ExitStack())
        ident = gctx.enter_context(nc.sbuf_tensor("ident", [128, 128], BF16))
        identf = gctx.enter_context(nc.sbuf_tensor("identf", [128, 128], F32))
        cst = gctx.enter_context(nc.sbuf_tensor("cst", [128, 8], F32))
        eid = p.dma("sync", ident[:], ident_h, "ld0")
        p.dma("gpsimd", identf[:], ident_h, "ldg")
        ecs = p.dma("sync", cst[:], fconst_h, "ld1")
        p.wait("tensor", eid, p.ev("ldg"))
        p.wait("scalar", ecs)
        p.wait("vector", ecs)

        if need_a:
            with ExitStack() as actx:
                t = alloc_ffn_tensors(p, actx, pre='A')
                t["ident"] = ident
                t["cst"] = cst
                dstA = x1 if (need_b or need_c or need_d) else out
                emit_ffn(p, t, hs, dstA, wg_h, wu_h, wd_h, tag="a")
                p.barrier()
        if need_b:
            with ExitStack() as kctx:
                keep = {}
                ksb = lambda name, shape, dt_: kctx.enter_context(
                    nc.sbuf_tensor(name, shape, dt_))
                keep["S_q"] = ksb("kSq", [64, 2048], F32)
                keep["R_q"] = ksb("kRq", [64, 2048], F32)
                keep["R_att"] = ksb("kRa", [64, 2048], F32)
                keep["cq_tm"] = ksb("kcq", [64, 1024], BF16)
                keep["bsg_tm"] = ksb("kbsg", [64, 1024], BF16)
                keep["kdsc_tm"] = ksb("kkd", [64, 1024], BF16)
                keep["beta_tm"] = ksb("kbeta", [64, 1024], F32)
                keep["eglr"] = ksb("keglr", [1, 1024], F32)
                keep["eglrb"] = ksb("keglrb", [1, 1024], BF16)
                d["keep"] = keep
                d["ident"] = ident
                d["identf"] = identf
                d["cst"] = cst
                if "B1" in stages:
                    with ExitStack() as b1ctx:
                        d["ctx"] = b1ctx
                        emit_b1(p, d)
                        p.barrier()
                    if "dbgkeep" in dbg:
                        dk1 = dram("dbg_Sq", [64, 2048], F32, "ExternalOutput")
                        dk2 = dram("dbg_Rq", [64, 2048], F32, "ExternalOutput")
                        dk3 = dram("dbg_Ra", [64, 2048], F32, "ExternalOutput")
                        dk4 = dram("dbg_tm", [64, 1024 * 4], F32, "ExternalOutput")
                        dk5 = dram("dbg_egl", [1, 1024], F32, "ExternalOutput")
                        sdb = p.sem("sdbg")
                        p.dma("sync", dk1, keep["S_q"][:], sdb)
                        p.dma("sync", dk2, keep["R_q"][:], sdb)
                        p.dma("sync", dk3, keep["R_att"][:], sdb)
                        sdbg2 = p.sem("sdbg2")
                        p.dma("gpsimd", dk4.rearrange("p (k x) -> p k x", k=4)[:, 0],
                              keep["cq_tm"][:], sdbg2)
                        p.dma("gpsimd", dk4.rearrange("p (k x) -> p k x", k=4)[:, 1],
                              keep["bsg_tm"][:], sdbg2)
                        p.dma("gpsimd", dk4.rearrange("p (k x) -> p k x", k=4)[:, 2],
                              keep["kdsc_tm"][:], sdbg2)
                        p.dma("sync", dk4.rearrange("p (k x) -> p k x", k=4)[:, 3],
                              keep["beta_tm"][:], sdb)
                        p.wait("sync", p.ev("sdbg2"))
                        p.dma("sync", dk5, keep["eglr"][:], sdb)
                        p.wait("sync", p.ev("sdbg"))
                if "B2" in stages:
                    with ExitStack() as b2ctx:
                        d["ctx"] = b2ctx
                        emit_b2(p, d)
                        p.barrier()
        if need_c:
            with ExitStack() as cctx:
                d["ctx"] = cctx
                d["ident"] = ident
                d["cst"] = cst
                emit_c(p, d)
                p.barrier()
        if need_d:
            with ExitStack() as dctx:
                t = alloc_ffn_tensors(p, dctx, pre='Dx')
                t["ident"] = ident
                t["cst"] = cst
                emit_ffn(p, t, x2, out, wg_h, wu_h, wd_h, tag="d")
        block = p.ctx.enter_context(nc.Block())
        p.finalize(block)
    return nc


def host_consts():
    ident = np.eye(128, dtype=ml_dtypes.bfloat16)
    colsel = np.zeros((128, 16 * 32), dtype=ml_dtypes.bfloat16)
    for h in range(16):
        colsel[:, h * 32 + 2 * h] = 1
        colsel[:, h * 32 + 2 * h + 1] = 1
    headsel = np.zeros((32, 2048), dtype=np.float32)
    for h in range(32):
        headsel[h, h * 64:(h + 1) * 64] = 1
    chunksel = np.zeros((32, 2048), dtype=np.float32)
    for ci in range(32):
        chunksel[ci, ci * 64:(ci + 1) * 64] = 1
    identw = np.tile(np.eye(64, dtype=ml_dtypes.bfloat16), (1, 32))
    ones1b = np.ones((1, 128), dtype=ml_dtypes.bfloat16)
    return dict(c_ident=ident, c_fconst=FCONST, colsel=colsel, headsel=headsel,
                chunksel=chunksel, identw=identw, ones1b=ones1b)


def host_attn_prep(W_qkv, W_b, W_a, conv_w, dt_bias, A_log):
    wqkv_c = np.ascontiguousarray(
        np.asarray(W_qkv, dtype=np.float32).reshape(D, 64, 128).transpose(1, 0, 2))
    wba = np.ascontiguousarray(
        np.concatenate([np.asarray(W_b, np.float32),
                        np.asarray(W_a, np.float32)], axis=1))
    convw = np.ascontiguousarray(
        np.asarray(conv_w, np.float32).reshape(64, 128, 4).transpose(1, 0, 2)
        .reshape(128, 256))
    dtb = np.ascontiguousarray(np.tile(np.asarray(dt_bias, np.float32), (128, 1)))
    alog_rep = np.ascontiguousarray(np.tile(np.asarray(A_log, np.float32),
                                            (128, 1)))
    return dict(wqkv_c=wqkv_c, wba=wba, convw=convw, dtb=dtb, alog_rep=alog_rep)


def kernel(L0_attn_hidden_states, L0_attn_W_qkv, L0_attn_W_z, L0_attn_W_b,
           L0_attn_W_a, L0_attn_conv_w, L0_attn_dt_bias, L0_attn_A_log,
           L0_attn_norm_w, L0_attn_W_out, L0_ffn_w_gate, L0_ffn_w_up,
           L0_ffn_w_down):
    nc = build_program(stages=ALL_STAGES)
    base = host_consts()
    base.update(host_attn_prep(L0_attn_W_qkv, L0_attn_W_b, L0_attn_W_a,
                               L0_attn_conv_w, L0_attn_dt_bias, L0_attn_A_log))
    base["ffn_wg"] = np.ascontiguousarray(L0_ffn_w_gate, dtype=np.float32)
    base["ffn_wu"] = np.ascontiguousarray(L0_ffn_w_up, dtype=np.float32)
    base["ffn_wd"] = np.ascontiguousarray(L0_ffn_w_down, dtype=np.float32)
    base["wz"] = np.ascontiguousarray(L0_attn_W_z, dtype=np.float32)
    base["wout"] = np.ascontiguousarray(L0_attn_W_out, dtype=np.float32)
    base["normw"] = np.ascontiguousarray(
        np.asarray(L0_attn_norm_w, np.float32).reshape(128, 1))
    in_maps = []
    for c in range(8):
        m = dict(base)
        m["hs"] = np.ascontiguousarray(L0_attn_hidden_states[c], dtype=np.float32)
        in_maps.append(m)
    res = run_bass_kernel_spmd(nc, in_maps, core_ids=list(range(8)))
    return np.stack([r["out"] for r in res.results], axis=0)


# version 16
# speedup vs baseline: 1.5463x; 1.5463x over previous
import numpy as np
import ml_dtypes
from contextlib import ExitStack

import concourse.bass as bass
import concourse.mybir as mybir
from concourse.bass_utils import run_bass_kernel_spmd

F32 = mybir.dt.float32
BF16 = mybir.dt.bfloat16
AF = mybir.ActivationFunctionType
ALU = mybir.AluOpType
AX = mybir.AxisListType

S = 2048
D = 2048
I = 8192
BLK = 512
NTS = BLK // 128
NBLK = S // BLK
NISL = I // 512
NK, NV, DK, DV = 16, 32, 128, 128
CHUNK = 64
NCHUNK = S // CHUNK
LN_EPS = 1e-5
CORE_EPS = 1e-6
L2_EPS = 1e-6

ENGS = ("sync", "tensor", "vector", "scalar", "gpsimd")


class Prog:

    def __init__(self, nc):
        self.nc = nc
        self.q = {e: [] for e in ENGS}
        self.sems = {}
        self.cnt = {}
        self.ctx = ExitStack()

    def sem(self, name):
        self.sems[name] = self.ctx.enter_context(self.nc.semaphore(name))
        self.cnt[name] = 0
        return name

    @staticmethod
    def _flat(evs):
        out = []
        for e in evs:
            if e is None:
                continue
            if isinstance(e, tuple):
                out.append(e)
            else:
                out.extend(x for x in e if x is not None)
        best = {}
        for (s, v) in out:
            best[s] = max(best.get(s, 0), v)
        return tuple(best.items())

    def do(self, eng, meth, *a, waits=(), inc=None, incn=1, **kw):
        ev = None
        if inc is not None:
            self.cnt[inc] += incn
            ev = (inc, self.cnt[inc])
        sems = self.sems
        waits = self._flat(waits)

        def fn(E, meth=meth, a=a, kw=kw, waits=waits, inc=inc, incn=incn):
            for i_, (s, v) in enumerate(waits):
                if i_ and i_ % 2 == 0:
                    E.nop()
                E.wait_ge(sems[s], v)
            ins = getattr(E, meth)(*a, **kw)
            if inc is not None:
                ins.then_inc(sems[inc], incn)
        self.q[eng].append(fn)
        return ev

    def wait(self, eng, *evs):
        sems = self.sems
        evs = self._flat(evs)
        if not evs:
            return

        def fn(E, evs=evs):
            for i_, (s, v) in enumerate(evs):
                if i_ and i_ % 2 == 0:
                    E.nop()
                E.wait_ge(sems[s], v)
        self.q[eng].append(fn)

    def dma(self, eng, out, in_, sem, waits=()):
        return self.do(eng, "dma_start", waits=waits, inc=sem, incn=16,
                       out=out, in_=in_)

    def ev(self, sem):
        return (sem, self.cnt[sem])

    def barrier(self, name="bar"):
        if name not in self.sems:
            self.sem(name)
        h = self.sems[name]
        for e in ENGS:
            self.cnt[name] += 1
            self.q[e].append(lambda E, h=h: E.sem_inc(h, 1))
        tgt = self.cnt[name]
        for e in ENGS:
            self.wait(e, (name, tgt))

    def finalize(self, block):
        for e in ENGS:
            ops = self.q[e]

            def runner(E, ops=ops):
                for fn in ops:
                    fn(E)
            getattr(block, e)(runner)


def bf16(x):
    return np.ascontiguousarray(x).astype(ml_dtypes.bfloat16)


FCONST = np.tile(np.array([LN_EPS, CORE_EPS, 1.0, 0.0, 0.0, 0.0, 0.0, 0.0],
                          dtype=np.float32), (128, 1))


def emit_ln(p, xt_ap, sq_ap, mv2, lnt, rstd, nbias, cst, yt_ap,
            sq_waits=(), yt_war=None, x_ev=None):
    pe, ve, ac = "pe", "ve", "ac"
    evq2 = p.do("scalar", "activation", sq_ap, xt_ap, AF.Square,
                waits=[x_ev] + list(sq_waits), inc=ac)
    ev1 = p.do("vector", "tensor_reduce", mv2[:, 0:1], xt_ap, AX.X, ALU.add,
               waits=[x_ev], inc=ve)
    ev2 = p.do("vector", "tensor_reduce", mv2[:, 1:2], sq_ap, AX.X, ALU.add,
               waits=[evq2], inc=ve)
    ev3 = p.do("vector", "tensor_scalar_mul", mv2[:, 0:1], mv2[:, 0:1],
               float(1.0 / D), waits=[ev1], inc=ve)
    ev4 = p.do("vector", "tensor_mul", lnt, mv2[:, 0:1], mv2[:, 0:1],
               waits=[ev3], inc=ve)
    ev5 = p.do("vector", "scalar_tensor_tensor", mv2[:, 1:2], mv2[:, 1:2],
               float(1.0 / D), lnt, op0=ALU.mult, op1=ALU.subtract,
               waits=[ev2, ev4], inc=ve)
    ev6 = p.do("scalar", "activation", lnt, mv2[:, 1:2], AF.Ln,
               bias=cst[:, 0:1], waits=[ev5], inc=ac)
    ev7 = p.do("scalar", "activation", rstd, lnt, AF.Exp, scale=-0.5,
               waits=[ev6], inc=ac)
    ev8 = p.do("vector", "scalar_tensor_tensor", nbias, mv2[:, 0:1], -1.0,
               rstd, op0=ALU.mult, op1=ALU.mult, waits=[ev7, ev3], inc=ve)
    evy = p.do("vector", "scalar_tensor_tensor", yt_ap, xt_ap,
               rstd, nbias.broadcast_to([128, 2048]),
               op0=ALU.mult, op1=ALU.add, waits=[ev8, yt_war], inc=ve)
    return evy


def emit_ffn(p, t, src, dst, wg_h, wu_h, wd_h, tag, cache=None, cache_mode=None):
    nc = p.nc
    lx = [p.sem(f"lx{tag}{i}") for i in range(2)]
    wgt = [p.sem(f"wg{tag}{i}") for i in range(2)]
    wut = [p.sem(f"wu{tag}{i}") for i in range(2)]
    wdt = [p.sem(f"wd{tag}{i}") for i in range(2)]
    st = [p.sem(f"st{tag}{i}") for i in range(2)]
    wcg = p.sem(f"wcg{tag}")
    wcd = p.sem(f"wcd{tag}")
    wgc, wuc, wdc = cache if cache else (None, None, None)
    pe, ve, ac, po = "pe", "ve", "ac", "po"

    xt, yt, yfm = t["xt"], t["yt"], t["yfm"]
    wg, wu, wd = t["wg"], t["wu"], t["wd"]
    hscr, habs, tre, cmp_, hs_tm, hst = (t["hscr"], t["habs"], t["tre"],
                                         t["cmp"], t["hs_tm"], t["hst"])
    res = t["res"]
    st6, mv2, rstd, nbias, lnt = t["st6"], t["mv2"], t["rstd"], t["nbias"], t["lnt"]
    ident = t["ident"]
    pg, pu, ptr, pf = t["pg"], t["pu"], t["ptr"], t["pf"]

    ev_yt_used = [None, None]
    ev_w_used = [None, None]
    ev_wd_used = [None, None]
    ev_ptr_ev = None
    ev_hst_used = None
    ev_xt_used = [None, None]
    ev_res_used = [None, None]

    ev_habs_used = None
    ev_hsp_g = None
    for blk in range(NBLK):
        r0 = blk * BLK
        for ts in range(NTS):
            par = ts % 2
            exl = p.dma("sync", xt[par][:], src[r0 + ts * 128: r0 + (ts + 1) * 128, :],
                        lx[par], waits=[e for e in (ev_xt_used[par],) if e])
            evy = emit_ln(p, xt[par][:], hscr[:], mv2, lnt[:], rstd[:],
                          nbias[:], t["cst"], yt[0][:],
                          sq_waits=(ev_hsp_g, ev_habs_used),
                          yt_war=ev_yt_used[0], x_ev=exl)
            ev_xt_used[par] = evy
            p.wait("tensor", evy, ev_ptr_ev)
            for j in range(16):
                p.do("tensor", "transpose",
                     ptr[:, j * 128:(j + 1) * 128],
                     yt[0][:, j * 128:(j + 1) * 128], ident[:],
                     inc=(pe if j == 15 else None))
            ev_yt_used[0] = p.ev(pe)
            ev_ptr_ev = p.do(
                "vector", "tensor_copy",
                yfm[:].rearrange("p (j t) -> p j t", j=16)[:, :, ts * 128:(ts + 1) * 128],
                ptr[:].rearrange("p (j t) -> p j t", j=16),
                waits=[p.ev(pe)], inc=ve)

        rd_cache = (cache_mode == "read") or (cache_mode == "write" and blk > 0)
        ev_hsp = None
        for isl in range(NISL):
            par = isl % 2
            wsl = slice(isl * 512, (isl + 1) * 512)
            if rd_cache:
                gate_rdy = (wcg, 32 * 16) if (cache_mode == "write"
                                              and blk == 1 and isl < 2) else None
                ewg = p.dma("sync", wg[par][:], wgc[isl], wgt[par],
                            waits=[e for e in (ev_w_used[par], gate_rdy) if e])
                ewu = p.dma("scalar", wu[par][:], wuc[isl], wut[par],
                            waits=[e for e in (ev_w_used[par], gate_rdy) if e])
            else:
                ewg = p.dma("gpsimd",
                            wg[par][:].rearrange("p (c i) -> p c i", c=16),
                            wg_h[:, wsl].rearrange("(c p) i -> p c i", p=128),
                            wgt[par], waits=[e for e in (ev_w_used[par],) if e])
                ewu = p.dma("gpsimd",
                            wu[par][:].rearrange("p (c i) -> p c i", c=16),
                            wu_h[:, wsl].rearrange("(c p) i -> p c i", p=128),
                            wut[par])
                if cache_mode == "write":
                    ecw1 = p.dma("sync", wgc[isl], wg[par][:], wcg, waits=[ewg])
                    ecw2 = p.dma("scalar", wuc[isl], wu[par][:], wcg, waits=[ewu])
            ev_g = [None] * NTS
            ev_u = [None] * NTS
            for ts in range(NTS):
                pgx, pux = pg[ts % 2], pu[ts % 2]
                p.wait("tensor", ewg if ts == 0 else None,
                       ewu if ts == 0 else None,
                       ev_ptr_ev if (isl == 0 and ts == 0) else None,
                       t["ev_pg_free"][ts % 2], t["ev_pu_free"][ts % 2],
                       t["ev_pf_free"][ts % 2] if isl == 0 else None,
                       t["ev_pf_free"][2 + ts % 2] if isl == 0 else None)
                t["ev_pg_free"][ts % 2] = None
                t["ev_pu_free"][ts % 2] = None
                if isl == 0:
                    t["ev_pf_free"][ts % 2] = None
                    t["ev_pf_free"][2 + ts % 2] = None
                for c in range(16):
                    lhs = yfm[:, c * 512 + ts * 128: c * 512 + (ts + 1) * 128]
                    ev_g[ts] = p.do("tensor", "matmul", pgx[:], lhs,
                                    wg[par][:, c * 512:(c + 1) * 512],
                                    start=(c == 0), stop=(c == 15),
                                    inc=(pe if c == 15 else None))
                for c in range(16):
                    lhs = yfm[:, c * 512 + ts * 128: c * 512 + (ts + 1) * 128]
                    ev_u[ts] = p.do("tensor", "matmul", pux[:], lhs,
                                    wu[par][:, c * 512:(c + 1) * 512],
                                    start=(c == 0), stop=(c == 15),
                                    inc=(pe if c == 15 else None))
                if ts == NTS - 1:
                    ev_w_used[par] = (
                        [p.ev(pe), ecw1, ecw2]
                        if (cache_mode == "write" and not rd_cache)
                        else p.ev(pe))
                evs = p.do("scalar", "activation", hscr[:, ts * 512:(ts + 1) * 512],
                           pgx[:], AF.Silu, waits=[ev_g[ts]], inc=ac)
                t["ev_pg_free"][ts % 2] = evs
                evm = p.do("vector", "tensor_mul", hscr[:, ts * 512:(ts + 1) * 512],
                           hscr[:, ts * 512:(ts + 1) * 512], pux[:],
                           waits=[evs, ev_u[ts]], inc=ve)
                t["ev_pu_free"][ts % 2] = evm
            h4 = hscr[:].rearrange("p (x f) -> p x f", f=4)
            a = [h4[:, :, j] for j in range(4)]
            p.do("vector", "tensor_single_scalar",
                 habs[:].bitcast(mybir.dt.int32), hscr[:].bitcast(mybir.dt.int32),
                 0x7FFFFFFF, ALU.bitwise_and,
                 waits=[e for e in (ev_habs_used,) if e])
            hb4 = habs[:].rearrange("p (x f) -> p x f", f=4)
            b = [hb4[:, :, j] for j in range(4)]
            tr3 = tre[:].rearrange("p (k x) -> p k x", k=3)
            p.do("vector", "tensor_tensor", tr3[:, 0], b[0], b[1], ALU.max)
            p.do("vector", "tensor_tensor", tr3[:, 1], b[2], b[3], ALU.max)
            p.do("vector", "tensor_tensor", tr3[:, 2], tr3[:, 0], tr3[:, 1], ALU.min)
            p.do("vector", "tensor_tensor", tr3[:, 0], b[0], b[1], ALU.min)
            p.do("vector", "tensor_tensor", tr3[:, 1], b[2], b[3], ALU.min)
            p.do("vector", "tensor_tensor", tr3[:, 0], tr3[:, 0], tr3[:, 1], ALU.max)
            evt = p.do("vector", "tensor_tensor", tr3[:, 0], tr3[:, 2], tr3[:, 0],
                       ALU.max, inc=ve)
            cm4 = cmp_[:].rearrange("p (k x) -> p k x", k=4)
            evc = [None] * 4
            for j in range(4):
                evc[j] = p.do("vector", "tensor_tensor", cm4[:, j], b[j], tr3[:, 0],
                              ALU.is_ge, inc=(ve if j == 3 else None))
            ev_habs_used = evc[3]
            hsv = hs_tm[:].rearrange("p (x f) -> p x f", f=4)
            p.wait("gpsimd", ev_hst_used, evc[3])
            for j in range(4):
                ev_hsp = p.do("gpsimd", "tensor_tensor", hsv[:, :, j], a[j],
                              cm4[:, j], ALU.mult,
                              inc=(po if j == 3 else None))
            ev_hsp_g = ev_hsp
            p.wait("tensor", ev_hsp, ev_ptr_ev)
            hsq = hs_tm[:].rearrange("p (ts ic t) -> p ts ic t", ts=NTS, ic=4)
            for ic in range(4):
                for ts in range(NTS):
                    p.do("tensor", "transpose", ptr[:, (ic * 4 + ts) * 128:(ic * 4 + ts + 1) * 128],
                         hsq[:, ts, ic], ident[:],
                         inc=(pe if (ic == 3 and ts == 3) else None))
            ev_hst_used = p.ev(pe)
            ev_ptr_ev = p.do("vector", "tensor_copy",
                             hst[:, (isl * 4) * 512:(isl * 4 + 4) * 512],
                             ptr[:], waits=[p.ev(pe)], inc=ve)

        ev_evac = [None] * NTS
        ev_grp = {}
        ev_cwd = {}
        for qd in range(4):
            qs = slice(qd * 512, (qd + 1) * 512)
            for grp in range(16):
                par = grp % 2
                wd_war = ev_grp.get((qd, grp - 2))
                if grp < 2 and qd > 0:
                    wd_war = ev_grp.get((qd - 1, 14 + grp))
                cw_war = ev_cwd.get((qd, grp - 2))
                if grp < 2 and qd > 0:
                    cw_war = ev_cwd.get((qd - 1, 14 + grp))
                gsl = slice(grp * 4 * 512, (grp + 1) * 4 * 512)
                if rd_cache:
                    wd_rdy = (wcd, 64 * 16) if (cache_mode == "write"
                                                and blk == 1 and qd == 0
                                                and grp < 2) else None
                    ewd = p.dma("gpsimd", wd[par][:], wdc[qd][:, gsl], wdt[par],
                                waits=[e for e in (wd_war, wd_rdy) if e])
                else:
                    ewd = p.dma("gpsimd",
                                wd[par][:].rearrange("p (c f) -> p c f", c=4),
                                wd_h[grp * 4 * 128:(grp + 1) * 4 * 128, qs]
                                .rearrange("(c p) f -> p c f", p=128),
                                wdt[par],
                                waits=[e for e in (wd_war, cw_war) if e])
                    if cache_mode == "write":
                        ev_cwd[(qd, grp)] = p.dma("sync", wdc[qd][:, gsl],
                                                  wd[par][:], wcd, waits=[ewd])
                p.wait("tensor", ewd,
                       ev_ptr_ev if (qd == 0 and grp == 0) else None)
                if grp == 0 and qd == 0:
                    p.wait("tensor",
                           t["ev_pg_free"][0], t["ev_pg_free"][1],
                           t["ev_pu_free"][0], t["ev_pu_free"][1])
                    t["ev_pg_free"] = [None, None]
                    t["ev_pu_free"] = [None, None]
                if grp == 0 and qd > 0:
                    p.wait("tensor", *[e for e in ev_evac if e])
                for j in range(4):
                    ich = grp * 4 + j
                    for ts in range(NTS):
                        ev = p.do("tensor", "matmul", pf[ts][:],
                                  hst[:, (ich * 4 + ts) * 128:(ich * 4 + ts + 1) * 128],
                                  wd[par][:, j * 512:(j + 1) * 512],
                                  start=(ich == 0), stop=(ich == 63),
                                  inc=(pe if (j == 3 and ts == NTS - 1) else None))
                ev_grp[(qd, grp)] = ev
            ev_mm_done = ev_grp[(qd, 15)]
            for ts in range(NTS):
                par = ts % 2
                exl = p.dma("sync", xt[par][:, 0:512],
                            src[r0 + ts * 128: r0 + (ts + 1) * 128, qs],
                            lx[par], waits=[e for e in (ev_xt_used[par],) if e])
                eve = p.do("vector", "scalar_tensor_tensor", res[par][:],
                           pf[ts][:], 0.5, xt[par][:, 0:512],
                           op0=ALU.mult, op1=ALU.add,
                           waits=[exl, ev_mm_done] + [e for e in (ev_res_used[par],) if e],
                           inc=ve)
                ev_evac[ts] = eve
                t["ev_pf_free"][ts] = eve
                ev_xt_used[par] = eve
                ev_res_used[par] = p.dma("sync", dst[r0 + ts * 128: r0 + (ts + 1) * 128, qs],
                                         res[par][:], st[par], waits=[eve])
        ev_xt_used = [ev_xt_used[0], ev_xt_used[1]]
    p.wait("sync", p.ev(st[0]), p.ev(st[1]), p.ev(wcg), p.ev(wcd))
    return [p.ev(st[0]), p.ev(st[1])]


def alloc_ffn_tensors(p, ctx, pre=""):
    nc = p.nc
    t = {}
    t["xt"] = [ctx.enter_context(nc.sbuf_tensor(pre + f"xt{i}", [128, 2048], F32))
               for i in range(2)]
    t["yt"] = [ctx.enter_context(nc.sbuf_tensor(pre + "yt0", [128, 2048], BF16))]
    t["yfm"] = ctx.enter_context(nc.sbuf_tensor(pre + "yfm", [128, 16 * 512], BF16))
    t["wg"] = [ctx.enter_context(nc.sbuf_tensor(pre + f"wg{i}", [128, 16 * 512], BF16))
               for i in range(2)]
    t["wu"] = [ctx.enter_context(nc.sbuf_tensor(pre + f"wu{i}", [128, 16 * 512], BF16))
               for i in range(2)]
    t["wd"] = [ctx.enter_context(nc.sbuf_tensor(pre + f"wd{i}", [128, 4 * 512], BF16))
               for i in range(2)]
    t["hscr"] = ctx.enter_context(nc.sbuf_tensor(pre + "hscr", [128, 4 * 512], F32))
    t["habs"] = ctx.enter_context(nc.sbuf_tensor(pre + "habs", [128, 4 * 512], F32))
    t["tre"] = ctx.enter_context(nc.sbuf_tensor(pre + "tre", [128, 3 * 512], F32))
    t["cmp"] = ctx.enter_context(nc.sbuf_tensor(pre + "cmp", [128, 4 * 512], BF16))
    t["hs_tm"] = ctx.enter_context(nc.sbuf_tensor(pre + "hs_tm", [128, 4 * 512], BF16))
    t["hst"] = ctx.enter_context(nc.sbuf_tensor(pre + "hst", [128, 64 * 4 * 128], BF16))
    t["res"] = [ctx.enter_context(nc.sbuf_tensor(pre + f"res{i}", [128, 512], F32))
                for i in range(2)]
    t["st6"] = ctx.enter_context(nc.sbuf_tensor(pre + "st6", [128, 24], F32))
    t["mv2"] = ctx.enter_context(nc.sbuf_tensor(pre + "mv2", [128, 2], F32))
    t["rstd"] = ctx.enter_context(nc.sbuf_tensor(pre + "rstd", [128, 1], F32))
    t["nbias"] = ctx.enter_context(nc.sbuf_tensor(pre + "nbias", [128, 1], F32))
    t["lnt"] = ctx.enter_context(nc.sbuf_tensor(pre + "lnt", [128, 1], F32))
    ps = [ctx.enter_context(nc.psum_tensor(pre + f"ps{i}", [128, 512], F32))
          for i in range(4)]
    t["pg"] = [ps[0], ps[1]]
    t["pu"] = [ps[2], ps[3]]
    t["pf"] = ps
    t["ptr"] = ctx.enter_context(nc.psum_tensor(pre + "ptr", [128, 16 * 128], BF16))
    t["ev_pg_free"] = [None, None]
    t["ev_pu_free"] = [None, None]
    t["ev_pf_free"] = [None, None, None, None]
    return t


def emit_b1(p, d, tag="b1"):
    nc = p.nc
    pe, ve, ac, po = "pe", "ve", "ac", "po"
    lx = [p.sem(f"{tag}lx{i}") for i in range(2)]
    wq = [p.sem(f"{tag}wq{i}") for i in range(2)]
    stq = p.sem(f"{tag}stq")
    stt_ = [p.sem(f"{tag}stt{i}") for i in range(2)]
    stb = p.sem(f"{tag}stb")
    sty = p.sem(f"{tag}sty")
    ld2 = p.sem(f"{tag}ld2")
    ld3 = p.sem(f"{tag}ld3")
    ldw = p.sem(f"{tag}ldw")
    ctx = d["ctx"]
    ident = d["ident"]
    identf = d["identf"]
    cst = d["cst"]
    keep = d["keep"]

    sb = lambda name, shape, dt_: ctx.enter_context(nc.sbuf_tensor(name, shape, dt_))
    psum = lambda name, shape, dt_: ctx.enter_context(nc.psum_tensor(name, shape, dt_))
    convw = sb("bconvw", [128, 256], F32)
    colsel = sb("bcolsel", [128, 16 * 32], BF16)
    dtb = sb("bdtb", [128, 32], F32)
    nega = sb("bnega", [128, 32], F32)
    gsc1 = sb("bgsc1", [128, 32], F32)
    gsc2 = sb("bgsc2", [128, 32], F32)
    beta_tm = keep["beta_tm"]
    g_tm = sb("g_tm", [128, 16 * 32], F32)
    sst_ = sb("bsst", [64, 2048], F32)
    sstq = sst_[0:32, :]
    sstk = sst_[32:64, :]
    s1ctx = ExitStack()
    s1 = lambda name, shape, dt_: s1ctx.enter_context(
        nc.sbuf_tensor(name, shape, dt_))
    yfm = s1("yfm_b", [128, 16 * 2048], BF16)
    xt = [s1(f"bxt{i}", [128, 2048], F32) for i in range(2)]
    yt = s1("byt", [128, 2048], BF16)
    wqs = [s1(f"wqs{i}", [128, 16 * 128], BF16) for i in range(2)]
    mrow = s1("mrow", [128, 2051], F32)
    acc = s1("bacc", [128, 2048], F32)
    acc2 = s1("bacc2", [128, 2048], F32)
    silf = acc
    qkvt = [s1(f"bqkvt{i}", [128, 2048], BF16) for i in range(2)]
    sqscr = [s1(f"bsq{i}", [128, 2048], BF16) for i in range(2)]
    tmst = [s1(f"btmst{i}", [128, 2048], BF16) for i in range(2)]
    wba = s1("bwba", [128, 16 * 64], BF16)
    st6 = s1("bst6", [128, 24], F32)
    mv2 = s1("bmv2", [128, 2], F32)
    rstd = s1("brstd", [128, 1], F32)
    nbias = s1("bnbias", [128, 1], F32)
    lnt = s1("blnt", [128, 1], F32)

    pm = [psum(f"bpm{i}", [128, 512], F32) for i in range(2)]
    ptr = psum("bptr", [128, 16 * 128], BF16)

    p.dma("sync", convw[:], d["convw"], ld2)
    p.dma("sync", colsel[:], d["colsel"], ld2)
    p.dma("gpsimd", wba[:].rearrange("p (c i) -> p c i", c=16),
          d["wba"].rearrange("(c p) i -> p c i", p=128), ldw)
    p.dma("sync", dtb[:], d["dtb"], ld2)
    p.dma("sync", nega[:], d["alog_rep"], ld2)
    p.dma("sync", keep["S_q"][32:64, :], d["chunksel"], ld2)
    p.dma("sync", keep["R_q"][0:32, :], d["headsel"], ld2)
    eco = p.dma("sync", keep["R_att"][0:32, :], d["headsel"], ld2)
    p.do("gpsimd", "memset", mrow[:, 0:3], 0.0)
    ev_ms = p.do("gpsimd", "memset", mrow[:, 0:3], 0.0, inc=po)
    eea = p.do("scalar", "activation", nega[:], nega[:], AF.Exp, waits=[eco],
               inc=ac)
    ena = p.do("vector", "tensor_scalar_mul", nega[:], nega[:], -1.0, waits=[eea],
               inc=ve)

    ev_xt_used = [None, None]
    ev_yt_used = None
    ev_ptr_ev = None
    for tt in range(16):
        par = tt % 2
        exl = p.dma("sync", xt[par][:], d["x1"][tt * 128:(tt + 1) * 128, :],
                    lx[par], waits=[ev_xt_used[par]])
        evy = emit_ln(p, xt[par][:], acc[:], mv2, lnt[:], rstd[:],
                      nbias[:], cst, yt[:], yt_war=ev_yt_used, x_ev=exl)
        ev_xt_used[par] = evy
        p.wait("tensor", evy, ev_ptr_ev)
        for j in range(16):
            p.do("tensor", "transpose", ptr[:, j * 128:(j + 1) * 128],
                 yt[:, j * 128:(j + 1) * 128], ident[:],
                 inc=(pe if j == 15 else None))
        ev_yt_used = p.ev(pe)
        ev_ptr_ev = p.do(
            "vector", "tensor_copy",
            yfm[:].rearrange("p (j t) -> p j t", j=16)[:, :, tt * 128:(tt + 1) * 128],
            ptr[:].rearrange("p (j t) -> p j t", j=16),
            waits=[p.ev(pe)], inc=ve)
    ey2 = p.dma("sync", d["y2fm"], yfm[:], sty, waits=[ev_ptr_ev])

    ev_pm_free = [None, None]
    for tt in range(16):
        pbg = pm[tt % 2]
        p.wait("tensor", p.ev(ld2) if tt == 0 else None,
               p.ev(ldw) if tt == 0 else None,
               ev_ptr_ev if tt == 0 else None, ev_pm_free[tt % 2])
        ev_pm_free[tt % 2] = None
        for dc in range(16):
            evm = p.do("tensor", "matmul", pbg[:, 0:64],
                       yfm[:, dc * 2048 + tt * 128: dc * 2048 + (tt + 1) * 128],
                       wba[:, dc * 64:(dc + 1) * 64],
                       start=(dc == 0), stop=(dc == 15),
                       inc=(pe if dc == 15 else None))
        p.do("scalar", "activation", beta_tm[:, (2 * tt) * 32:(2 * tt + 1) * 32],
             pbg[0:64, 0:32], AF.Sigmoid, waits=[evm])
        evs = p.do("scalar", "activation",
                   beta_tm[:, (2 * tt + 1) * 32:(2 * tt + 2) * 32],
                   pbg[64:128, 0:32], AF.Sigmoid, inc=ac)
        ev1 = p.do("vector", "tensor_add", gsc1[:], pbg[:, 32:64], dtb[:],
                   waits=[evm, ena], inc=ve)
        evab = p.do("vector", "tensor_single_scalar",
                    gsc2[:].bitcast(mybir.dt.int32), gsc1[:].bitcast(mybir.dt.int32),
                    0x7FFFFFFF, ALU.bitwise_and, waits=[ev1], inc=ve)
        ev2 = p.do("scalar", "activation", gsc2[:], gsc2[:], AF.Exp, scale=-1.0,
                   waits=[evab], inc=ac)
        ev3 = p.do("scalar", "activation", gsc2[:], gsc2[:], AF.Ln,
                   bias=cst[:, 2:3], waits=[ev2], inc=ac)
        ev4 = p.do("vector", "tensor_scalar_max", gsc1[:], gsc1[:], 0.0,
                   waits=[ev3], inc=ve)
        ev5 = p.do("vector", "tensor_add", gsc1[:], gsc1[:], gsc2[:],
                   waits=[ev4], inc=ve)
        evg = p.do("vector", "tensor_mul", g_tm[:, tt * 32:(tt + 1) * 32],
                   gsc1[:], nega[:], waits=[ev5], inc=ve)
        ev_pm_free[tt % 2] = [evg, evs]

    ev_wq_used = [None, None]
    ev_qkv_used = [None, None]
    ev_sq_used = [None, None]
    ev_tm_used = [None, None]
    ev_ss_q = None
    ev_acc_used = None
    pss_ctx = ExitStack()
    pss = pss_ctx.enter_context(nc.psum_tensor("bpss", [32, 2048], F32))
    for cc in range(64):
        par = cc % 2
        ewq = p.dma("gpsimd", wqs[par][:].rearrange("p (c i) -> p c i", c=16),
                    d["wqkv_c"][cc].rearrange("(c p) i -> p c i", p=128),
                    wq[par], waits=[ev_wq_used[par]])
        for tb in range(4):
            pmx = pm[tb % 2]
            p.wait("tensor", ewq if tb == 0 else None, ev_pm_free[tb % 2])
            ev_pm_free[tb % 2] = None
            for dc in range(16):
                evm = p.do("tensor", "matmul", pmx[:],
                           wqs[par][:, dc * 128:(dc + 1) * 128],
                           yfm[:, dc * 2048 + tb * 512: dc * 2048 + (tb + 1) * 512],
                           start=(dc == 0), stop=(dc == 15),
                           inc=(pe if dc == 15 else None))
            if tb == 3:
                ev_wq_used[par] = p.ev(pe)
            eve = p.do("vector", "tensor_copy",
                       mrow[:, 3 + tb * 512: 3 + (tb + 1) * 512], pmx[:],
                       waits=[evm, ev_ms if cc == 0 and tb == 0 else None],
                       inc=ve)
            ev_pm_free[tb % 2] = eve
        w_ = lambda j: convw[:, cc * 4 + j: cc * 4 + j + 1]
        p.do("vector", "tensor_scalar_mul", acc[:], mrow[:, 3:2051], w_(3),
             waits=[ev_acc_used, p.ev(ac) if cc == 0 else None])
        p.do("vector", "scalar_tensor_tensor", acc2[:], mrow[:, 2:2050], w_(2),
             acc[:], op0=ALU.mult, op1=ALU.add)
        p.do("vector", "scalar_tensor_tensor", acc[:], mrow[:, 1:2049], w_(1),
             acc2[:], op0=ALU.mult, op1=ALU.add)
        evcv = p.do("vector", "scalar_tensor_tensor", acc2[:], mrow[:, 0:2048],
                    w_(0), acc[:], op0=ALU.mult, op1=ALU.add, inc=ve)
        is_q = cc < 16
        is_k = 16 <= cc < 32
        if is_q or is_k:
            evsl = p.do("scalar", "activation", silf[:], acc2[:], AF.Silu,
                        waits=[evcv], inc=ac)
            evqk = p.do("vector", "tensor_copy", qkvt[par][:], silf[:],
                        waits=[evsl, ev_qkv_used[par]], inc=ve)
            evsq = p.do("scalar", "activation", sqscr[par][:], silf[:], AF.Square,
                        waits=[evqk, ev_sq_used[par]], inc=ac)
            hh = cc if is_q else cc - 16
            for n in range(4):
                evss = p.do("tensor", "matmul", pss[:, n * 512:(n + 1) * 512],
                            colsel[:, hh * 32:(hh + 1) * 32],
                            sqscr[par][:, n * 512:(n + 1) * 512],
                            start=(hh == 0), stop=(hh == 15),
                            waits=([evsq] if n == 0 else [])
                            + ([ev_ss_q] if (n == 0 and is_k and hh == 0) else []),
                            inc=(pe if n == 3 else None))
            ev_sq_used[par] = evss
            ev_acc_used = evsq
            dst = d["qfm_sp"] if is_q else d["kfm_sp"]
            esp = p.dma("sync", dst[hh], qkvt[par][:], stq, waits=[evqk])
            new_used = [esp]
        else:
            evqk = p.do("scalar", "activation", qkvt[par][:], acc2[:], AF.Silu,
                        waits=[evcv, ev_qkv_used[par]], inc=ac)
            ev_acc_used = evqk
            new_used = []
        if is_k or cc >= 32:
            p.wait("tensor", evqk, ev_ptr_ev)
            for j in range(16):
                p.do("tensor", "transpose", ptr[:, j * 128:(j + 1) * 128],
                     qkvt[par][:, j * 128:(j + 1) * 128], ident[:],
                     inc=(pe if j == 15 else None))
            new_used.append(p.ev(pe))
            ev_ptr_ev = p.do("vector", "tensor_copy", tmst[par][:], ptr[:],
                             waits=[p.ev(pe), ev_tm_used[par]], inc=ve)
            if is_k:
                out_ap = d["ktm_sp"].rearrange("(tt p) c -> p tt c", p=128)[
                    :, :, (cc - 16) * 128:(cc - 15) * 128]
            else:
                out_ap = d["vtm_sp"].rearrange("(tt p) c -> p tt c", p=128)[
                    :, :, (cc - 32) * 128:(cc - 31) * 128]
            ev_tm_used[par] = p.dma("sync", out_ap,
                                    tmst[par][:].rearrange("p (tt c) -> p tt c", tt=16),
                                    stt_[par], waits=[ev_ptr_ev])
        ev_qkv_used[par] = new_used
        if cc == 15:
            ev_ss_q = p.do("vector", "tensor_copy", sstq, pss[:],
                           waits=[p.ev(pe)], inc=ve)
        if cc == 31:
            ev_ss_k = p.do("vector", "tensor_copy", sstk, pss[:],
                           waits=[p.ev(pe)], inc=ve)
    p.wait("tensor", ev_ss_k)
    pss_ctx.close()
    drains = [p.ev(pe), p.ev(ac), p.ev(po), p.ev(ve), p.ev(sty), p.ev(stq),
              p.ev(stt_[0]), p.ev(stt_[1])]
    p.wait("vector", *drains)
    p.wait("scalar", *drains)
    p.wait("tensor", *drains)
    p.wait("gpsimd", *drains)
    s1ctx.close()
    ptf = ctx.enter_context(nc.psum_tensor("bptf", [64, 1024], F32))
    bgfm = sb("bgfm", [64, 2048], F32)
    gc_fm = sb("gc_fm", [32, 2048], F32)
    zeros_ = sb("zeros64", [64, 64], F32)
    zeros64 = zeros_[32:64, :]
    t1t = sb("bt1", [32, 2048], F32)
    t1f = t1t[:, :]
    t2t = sb("bt2", [32, 2048], F32)
    t2f = t2t[:, :]
    t3t = sb("bt3", [32, 2048], F32)
    t3f = t3t[:, :]
    kdlt = sb("bkdl", [32, 2048], F32)
    kdlf = kdlt[:, :]
    ls1t = sb("bls1", [32, 2048], F32)
    ls1 = ls1t[:, :]
    ls2t = sb("bls2", [32, 2048], F32)
    ls2 = ls2t[:, :]
    tmexp = sb("btmexp", [128, 512], F32)
    glc = sb("bglc", [32, 32], F32)
    p.do("gpsimd", "memset", zeros_[:], 0.0, inc=po)

    ev_half = None
    for qd in range(4):
        p.wait("tensor", p.ev(ac) if qd == 0 else None,
               p.ev(ve) if qd == 0 else None, ev_half)
        for cl in range(8):
            ci_ = qd * 8 + cl
            p.do("tensor", "transpose", ptf[0:32, cl * 64:(cl + 1) * 64],
                 beta_tm[:, ci_ * 32:(ci_ + 1) * 32], identf[0:64, 0:64],
                 inc=(pe if cl == 7 else None))
        ev_half = p.do("vector", "tensor_copy",
                       bgfm[0:32, qd * 512:(qd + 1) * 512],
                       ptf[0:32, 0:512], waits=[p.ev(pe)], inc=ve)
    for half in range(2):
        p.wait("tensor", ev_half)
        for tt in range(8 * half, 8 * half + 8):
            off = (tt - 8 * half) * 128
            p.do("tensor", "transpose", ptf[0:32, off:off + 128],
                 g_tm[:, tt * 32:(tt + 1) * 32], identf[:],
                 inc=(pe if tt == 8 * half + 7 else None))
        ev_half = p.do("vector", "tensor_copy",
                       bgfm[32:64, half * 1024:(half + 1) * 1024],
                       ptf[0:32, :], waits=[p.ev(pe)], inc=ve)
    for ci in range(NCHUNK):
        evgc = p.do("vector", "tensor_tensor_scan",
                    gc_fm[:, ci * 64:(ci + 1) * 64],
                    bgfm[32:64, ci * 64:(ci + 1) * 64], zeros64[:],
                    0.0, op0=ALU.add, op1=ALU.add,
                    inc=(ve if ci == NCHUNK - 1 else None))
    evl1 = p.do("scalar", "activation", ls1, sstk, AF.Ln,
                bias=cst[0:32, 1:2], waits=[ev_ss_k], inc=ac)
    evt2 = p.do("vector", "scalar_tensor_tensor", t2f, ls1, -0.5, gc_fm[:, :],
                op0=ALU.mult, op1=ALU.subtract, waits=[evl1, evgc], inc=ve)
    evls2 = p.do("vector", "scalar_tensor_tensor", ls2, ls1, -0.5, gc_fm[:, :],
                 op0=ALU.mult, op1=ALU.add, inc=ve)
    evlb = p.do("scalar", "activation", ls1, bgfm[0:32, :], AF.Ln,
                bias=cst[0:32, 3:4], waits=[evls2], inc=ac)
    evt1 = p.do("vector", "tensor_add", t1f, ls2, ls1, waits=[evlb],
                inc=ve)
    evl1b = p.do("scalar", "activation", ls1, sstk, AF.Ln,
                 bias=cst[0:32, 1:2], waits=[evt1], inc=ac)
    evglc = p.do("vector", "tensor_copy", glc[:],
                 gc_fm[:].rearrange("p (ci t) -> p ci t", ci=32)[:, :, 63],
                 waits=[evgc], inc=ve)
    p.do("vector", "tensor_tensor",
         ls2.rearrange("p (c t) -> p c t", c=32),
         glc[:].rearrange("p c -> p c ()").broadcast_to([32, 32, 64]),
         gc_fm[:].rearrange("p (c t) -> p c t", c=32), ALU.subtract,
         waits=[evglc])
    evkdl = p.do("vector", "scalar_tensor_tensor", kdlf, ls1, -0.5, ls2,
                 op0=ALU.mult, op1=ALU.add, waits=[evl1b], inc=ve)
    evl3 = p.do("scalar", "activation", ls1, sstq, AF.Ln,
                bias=cst[0:32, 1:2], waits=[ev_ss_q, evkdl], inc=ac)
    p.do("vector", "scalar_tensor_tensor", t3f, ls1, -0.5, gc_fm[:, :],
         op0=ALU.mult, op1=ALU.add, waits=[evl3])
    evt3 = p.do("vector", "tensor_scalar_add", t3f, t3f,
                float(-0.5 * np.log(128.0)), inc=ve)
    evsq2 = p.do("vector", "tensor_copy", keep["S_q"][0:32, :], t2f,
                 waits=[evt2], inc=ve)
    eb1 = p.dma("sync", d["tb"][0], t1f, stb, waits=[evt1])
    eb2 = p.dma("sync", d["tb"][1], t2f, stb, waits=[evt2])
    eb3 = p.dma("sync", d["tb"][2], t3f, stb, waits=[evt3])
    er1 = p.dma("sync", keep["R_q"][32:64, :],
                d["tb"][0].rearrange("h (ci f) -> ci h f", ci=32), ld3,
                waits=[eb3])
    er2 = p.dma("sync", keep["R_att"][32:64, :],
                d["tb"][2].rearrange("h (ci f) -> ci h f", ci=32), ld3)
    specs = [(t3f, keep["cq_tm"], evt3, 0), (t1f, keep["bsg_tm"], evt1, 0),
             (kdlf, keep["kdsc_tm"], evkdl, 0)]
    ev_pmt = [None, None]
    for i_, (srcf, dsttm, evin, bp) in enumerate(specs):
        pmx = pm[i_ % 2]
        p.wait("tensor", evin, ev_pmt[i_ % 2], ev_pm_free[i_ % 2])
        ev_pm_free[i_ % 2] = None
        for tt in range(16):
            p.do("tensor", "transpose", pmx[:, tt * 32:(tt + 1) * 32],
                 srcf[:, tt * 128:(tt + 1) * 128],
                 identf[bp:bp + 32, bp:bp + 32],
                 inc=(pe if tt == 15 else None))
        evtc = p.do("vector", "tensor_copy", tmexp[:], pmx[:],
                    waits=[p.ev(pe)], inc=ve)
        dv3 = dsttm[:].rearrange("p (c h) -> p c h", c=32)
        p.do("scalar", "activation", dv3[:, 0::2, :],
             tmexp[0:64, :].rearrange("p (t h) -> p t h", t=16), AF.Exp,
             waits=[evtc])
        ev_pmt[i_ % 2] = p.do("scalar", "activation", dv3[:, 1::2, :],
                              tmexp[64:128, :].rearrange("p (t h) -> p t h", t=16),
                              AF.Exp, inc=ac)
    p.wait("tensor", evglc, p.ev(ve))
    p.do("tensor", "transpose", ptf[0:32, 0:32], glc[:], identf[0:32, 0:32], inc=pe)
    evgt = p.do("vector", "tensor_copy", ls2[:, 0:32], ptf[0:32, 0:32],
                waits=[p.ev(pe)], inc=ve)
    ebg = p.dma("sync", d["glb"], ls2[0:32, 0:32], stb, waits=[evgt])
    erg = p.dma("sync", keep["eglr"][:],
                d["glb"].rearrange("a b -> () (a b)"), ld3, waits=[ebg])
    evex = p.do("scalar", "activation", keep["eglr"][:], keep["eglr"][:], AF.Exp,
                waits=[erg], inc=ac)
    p.do("vector", "tensor_copy", keep["eglrb"][:], keep["eglr"][:],
         waits=[evex], inc=ve)
    p.wait("sync", p.ev(stq), p.ev(stt_[0]), p.ev(stt_[1]), p.ev(sty),
           p.ev(stb), p.ev(ld3))
    return


def emit_b2(p, d, tag="b2"):
    nc = p.nc
    pe, ve, ac, po = "pe", "ve", "ac", "po"
    lq = [p.sem(f"{tag}lq{i}") for i in range(2)]
    lk = [p.sem(f"{tag}lk{i}") for i in range(2)]
    lt = [p.sem(f"{tag}lt{i}") for i in range(2)]
    lv = [p.sem(f"{tag}lv{i}") for i in range(2)]
    stc = p.sem(f"{tag}stc")
    ld4 = p.sem(f"{tag}ld4")
    ctx = d["ctx"]
    ident = d["ident"]
    identf = d["identf"]
    keep = d["keep"]
    sb = lambda name, shape, dt_: ctx.enter_context(nc.sbuf_tensor(name, shape, dt_))
    psum = lambda name, shape, dt_: ctx.enter_context(nc.psum_tensor(name, shape, dt_))

    qf = [sb(f"cqf{i}", [128, 1024], BF16) for i in range(2)]
    kf = [sb(f"ckf{i}", [128, 1024], BF16) for i in range(2)]
    ktm = [sb(f"cktm{i}", [64, 2048], BF16) for i in range(2)]
    vtm = sb("cvtm", [64, 4096], BF16)
    Qw = sb("cQw", [64, 2048], BF16)
    Qtw = sb("cQtw", [64, 2048], BF16)
    Aw = sb("cAw", [64, 2048], BF16)
    Em = sb("cEm", [64, 2048], BF16)
    Emm = sb("cEmm", [64, 2048], BF16)
    Dc = sb("cDc", [64, 1024], F32)
    Pdu = [sb(f"cPd{i}", [64, 2048], BF16) for i in range(4)]
    Pf = [sb(f"cPf{i}", [64, 2048], BF16) for i in range(2)]
    P32 = sb("cP32", [64, 2048], BF16)
    Ech = Pf
    identw = sb("cidw", [64, 2048], BF16)
    onesb = sb("cones", [1, 128], BF16)
    Vbw = sb("cVbw", [64, 4096], BF16)
    Kbgw = sb("cKbgw", [64, 4096], BF16)
    kdww = sb("ckdw", [64, 4096], BF16)
    usbw = sb("cusbw", [64, 4096], BF16)
    wtsw = sb("cwtsw", [128, 2048], BF16)
    wsb = sb("cwsb", [64, 512], F32)
    vns = [sb(f"cvns{i}", [64, 512], F32) for i in range(4)]
    vnb = [sb(f"cvnb{i}", [64, 512], BF16) for i in range(4)]
    osb = [sb(f"cosb{i}", [64, 512], F32) for i in range(2)]
    core_st = sb("ccst", [64, 4096], BF16)
    eglb = sb("ceglb", [128, 32], F32)
    Ssb = sb("cSsb", [128, 4096], F32)
    Sbf = sb("cSbf", [128, 4096], BF16)

    pA = psum("cpA", [64, 1024], F32)
    pB = psum("cpB", [64, 1024], F32)
    pC = psum("cpC", [64, 1024], F32)
    pT16 = psum("cpT", [64, 1024], BF16)
    pK = psum("cpK", [128, 512], F32)

    el1 = p.dma("sync", identw[:], d["identw"], ld4)
    el2 = p.dma("sync", onesb[:], d["ones1b"], ld4)
    p.do("gpsimd", "memset", Ssb[:], 0.0)
    evz = p.do("gpsimd", "memset", Sbf[:], 0.0, inc=po)
    p.wait("tensor", p.ev(ld4))

    free = {"pA": None, "pB": None, "pC": None, "pT": None, "pK": None}

    def use(which, *extra):
        p.wait("tensor", free[which], *extra)
        free[which] = None

    ev_ld_used = {"q": [None, None], "k": [None, None],
                  "t": [None, None], "v": None}
    ev_st_core = None
    ev_eglb_used = None
    ev_wsb_used = None
    ev_kd_used = None
    ev_uw_used = None
    v3 = lambda t_, n: t_[:].rearrange("p (h f) -> p h f", h=n)

    for ci in range(NCHUNK):
        par = ci % 2
        csl = slice(ci * 64, (ci + 1) * 64)
        eq = p.dma("sync", qf[par][:].rearrange("p (h t) -> p h t", h=16),
                   d["qfm_sp"].rearrange("h p t -> p h t")[:, :, csl],
                   lq[par], waits=[ev_ld_used["q"][par]])
        ek = p.dma("sync", kf[par][:].rearrange("p (h t) -> p h t", h=16),
                   d["kfm_sp"].rearrange("h p t -> p h t")[:, :, csl],
                   lk[par], waits=[ev_ld_used["k"][par]])
        et = p.dma("sync", ktm[par][:], d["ktm_sp"][csl, :], lt[par],
                   waits=[ev_ld_used["t"][par]])
        ev_ = p.dma("sync", vtm[:], d["vtm_sp"][csl, :], lv[0],
                    waits=[ev_ld_used["v"]])
        use("pK")
        p.do("tensor", "matmul", pK[:, 0:32], onesb[:],
             keep["eglrb"][:, ci * 32:(ci + 1) * 32], start=True, stop=True,
             inc=pe)
        ev_eglb = p.do("vector", "tensor_copy", eglb[:], pK[:, 0:32],
                       waits=[(p.ev(pe))] + ([ev_eglb_used] if ev_eglb_used else []),
                       inc=ve)
        free["pK"] = ev_eglb
        use("pA", ek, eq)
        for kh in range(16):
            p.do("tensor", "matmul", pA[:, kh * 64:(kh + 1) * 64],
                 kf[par][:, kh * 64:(kh + 1) * 64],
                 kf[par][:, kh * 64:(kh + 1) * 64], start=True, stop=True,
                 inc=(pe if kh == 15 else None))
        use("pB")
        for kh in range(16):
            p.do("tensor", "matmul", pB[:, kh * 64:(kh + 1) * 64],
                 kf[par][:, kh * 64:(kh + 1) * 64],
                 qf[par][:, kh * 64:(kh + 1) * 64], start=True, stop=True,
                 inc=(pe if kh == 15 else None))
        ev_ggq = p.ev(pe)
        ev_clamp = [None, None]
        for half in range(2):
            use("pC", p.ev(ve) if (ci == 0 and half == 0) else None)
            for n in range(2):
                evd = p.do("tensor", "matmul", pC[:, n * 512:(n + 1) * 512],
                           keep["S_q"][:, csl],
                           keep["R_q"][:, half * 1024 + n * 512:
                                       half * 1024 + (n + 1) * 512],
                           start=True, stop=True,
                           inc=(pe if n == 1 else None))
            ev_clamp[half] = p.do(
                "vector", "tensor_single_scalar", Dc[:], pC[:], 30.0, ALU.min,
                waits=[evd], inc=ve)
            free["pC"] = ev_clamp[half]
            p.do("scalar", "activation", Em[:, half * 1024:(half + 1) * 1024],
                 Dc[:], AF.Exp, waits=[ev_clamp[half]],
                 inc=(ac if half == 1 else None))
        evex = p.ev(ac)
        evmk = p.do("gpsimd", "affine_select", Emm[:], Em[:],
                    [[0, 32], [1, 64]], ALU.is_gt, 0.0,
                    base=0, channel_multiplier=-1, waits=[evex], inc=po)
        qv = Qw[:].rearrange("p (h f) -> p h f", h=32)
        emq3 = Emm[:].rearrange("p (h f) -> p h f", h=32)
        g3 = pA[:].rearrange("p (k f) -> p k f", k=16)
        p.do("vector", "scalar_tensor_tensor", qv[:, 0::2], g3, -1.0,
             emq3[:, 0::2], op0=ALU.mult, op1=ALU.mult,
             waits=[evmk, ev_ggq])
        evq = p.do("vector", "scalar_tensor_tensor", qv[:, 1::2], g3, -1.0,
                   emq3[:, 1::2], op0=ALU.mult, op1=ALU.mult, inc=ve)
        ev_emt = None
        for half in range(2):
            use("pT", evmk if half == 0 else None)
            for hh in range(16):
                h = half * 16 + hh
                p.do("tensor", "transpose", pT16[:, hh * 64:(hh + 1) * 64],
                     Emm[:, h * 64:(h + 1) * 64], ident[0:64, 0:64],
                     inc=(pe if hh == 15 else None))
            ev_emt = p.do("vector", "tensor_copy",
                          Emm[:, half * 1024:(half + 1) * 1024], pT16[:],
                          waits=[(p.ev(pe)), evq], inc=ve)
            free["pT"] = ev_emt
        qtv = Qtw[:].rearrange("p (h f) -> p h f", h=32)
        et_ = Emm[:].rearrange("p (h f) -> p h f", h=32)
        p.do("vector", "scalar_tensor_tensor", qtv[:, 0::2], g3, -1.0,
             et_[:, 0::2], op0=ALU.mult, op1=ALU.mult, waits=[ev_emt])
        evqt = p.do("vector", "scalar_tensor_tensor", qtv[:, 1::2], g3, -1.0,
                    et_[:, 1::2], op0=ALU.mult, op1=ALU.mult, inc=ve)
        free["pA"] = evqt
        ev_clamp = [None, None]
        for half in range(2):
            use("pC")
            for n in range(2):
                evd = p.do("tensor", "matmul", pC[:, n * 512:(n + 1) * 512],
                           keep["S_q"][:, csl],
                           keep["R_att"][:, half * 1024 + n * 512:
                                         half * 1024 + (n + 1) * 512],
                           start=True, stop=True,
                           inc=(pe if n == 1 else None))
            ev_clamp[half] = p.do(
                "vector", "tensor_single_scalar", Dc[:], pC[:], 30.0, ALU.min,
                waits=[evd, evex if half == 0 else None,
                       evqt if half == 0 else None], inc=ve)
            free["pC"] = ev_clamp[half]
            p.do("scalar", "activation", Em[:, half * 1024:(half + 1) * 1024],
                 Dc[:], AF.Exp, waits=[ev_clamp[half]],
                 inc=(ac if half == 1 else None))
        evex2 = p.ev(ac)
        evmk2 = p.do("gpsimd", "affine_select", Emm[:], Em[:],
                     [[0, 32], [1, 64]], ALU.is_ge, 0.0,
                     base=0, channel_multiplier=-1,
                     waits=[evex2, evqt], inc=po)
        av = Aw[:].rearrange("p (h f) -> p h f", h=32)
        em3 = Emm[:].rearrange("p (h f) -> p h f", h=32)
        gq3 = pB[:].rearrange("p (k f) -> p k f", k=16)
        p.do("vector", "tensor_tensor", av[:, 0::2], gq3, em3[:, 0::2],
             ALU.mult, waits=[evmk2])
        evaw = p.do("vector", "tensor_tensor", av[:, 1::2], gq3, em3[:, 1::2],
                    ALU.mult, inc=ve)
        free["pB"] = evaw
        cur_t, cur_f = Qtw, Qw
        for lvl in range(5):
            dst_f = P32 if lvl == 4 else Pf[lvl % 2]
            for half in range(2):
                use("pC", (evqt if lvl == 0 and half == 0 else None))
                for hh in range(16):
                    h = half * 16 + hh
                    p.do("tensor", "matmul", pC[:, hh * 64:(hh + 1) * 64],
                         cur_t[:, h * 64:(h + 1) * 64],
                         cur_f[:, h * 64:(h + 1) * 64], start=True, stop=True,
                         inc=(pe if hh == 15 else None))
                evf = p.do("vector", "tensor_copy",
                           dst_f[:, half * 1024:(half + 1) * 1024], pC[:],
                           waits=[p.ev(pe)], inc=ve)
                free["pC"] = evf
                if lvl < 4:
                    use("pA")
                    for hh in range(16):
                        h = half * 16 + hh
                        p.do("tensor", "matmul", pA[:, hh * 64:(hh + 1) * 64],
                             cur_f[:, h * 64:(h + 1) * 64],
                             cur_t[:, h * 64:(h + 1) * 64], start=True,
                             stop=True, inc=(pe if hh == 15 else None))
                    evd_ = p.do("vector", "tensor_copy",
                                Pdu[lvl][:, half * 1024:(half + 1) * 1024],
                                pA[:], waits=[p.ev(pe)], inc=ve)
                    free["pA"] = evd_
            if lvl < 4:
                cur_t, cur_f = Pdu[lvl], dst_f
        ev_pw = p.ev(ve)
        evch = p.do("vector", "tensor_add", Ech[0][:], P32[:], identw[:],
                    waits=[ev_pw], inc=ve)
        chain = [Pdu[3], Pdu[2], Pdu[1], Pdu[0], Qtw]
        cur = 0
        for Pt_ in chain:
            nxt = 1 - cur
            for half in range(2):
                use("pB", evch if half == 0 else None)
                for hh in range(16):
                    h = half * 16 + hh
                    p.do("tensor", "matmul", pB[:, hh * 64:(hh + 1) * 64],
                         Pt_[:, h * 64:(h + 1) * 64],
                         Ech[cur][:, h * 64:(h + 1) * 64], start=True,
                         stop=True, inc=(pe if hh == 15 else None))
                evch = p.do("vector", "tensor_add",
                            Ech[nxt][:, half * 1024:(half + 1) * 1024], pB[:],
                            Ech[cur][:, half * 1024:(half + 1) * 1024],
                            waits=[p.ev(pe)], inc=ve)
                free["pB"] = evch
            cur = nxt
        Fw = Ech[cur]
        ev_F = evch
        def sclw(t_, u_):
            return t_[:, ci * 32 + u_:ci * 32 + 32:2].rearrange(
                "p h -> p h ()").broadcast_to([64, 16, 128])

        kt3 = ktm[par][:].rearrange("p (k dd) -> p k dd", k=16)
        evvb1 = p.do("vector", "tensor_tensor",
                     v3(Vbw, 32)[:, 0::2], v3(vtm, 32)[:, 0::2],
                     sclw(keep["beta_tm"], 0), ALU.mult,
                     waits=[ev_, ev_uw_used], inc=ve)
        evvb = p.do("vector", "tensor_tensor",
                    v3(Vbw, 32)[:, 1::2], v3(vtm, 32)[:, 1::2],
                    sclw(keep["beta_tm"], 1), ALU.mult, inc=ve)
        p.do("vector", "tensor_tensor", v3(Kbgw, 32)[:, 0::2], kt3,
             sclw(keep["bsg_tm"], 0), ALU.mult, waits=[et])
        evkb = p.do("vector", "tensor_tensor", v3(Kbgw, 32)[:, 1::2], kt3,
                    sclw(keep["bsg_tm"], 1), ALU.mult, inc=ve)
        p.do("vector", "tensor_tensor", v3(kdww, 32)[:, 0::2], kt3,
             sclw(keep["kdsc_tm"], 0), ALU.mult, waits=[ev_kd_used])
        evkd = p.do("vector", "tensor_tensor", v3(kdww, 32)[:, 1::2], kt3,
                    sclw(keep["kdsc_tm"], 1), ALU.mult, inc=ve)
        for grp in range(4):
            use("pA", ev_F if grp == 0 else None,
                evvb if grp == 0 else None)
            for j in range(8):
                h = grp * 8 + j
                p.do("tensor", "matmul", pA[:, j * 128:(j + 1) * 128],
                     Fw[:, h * 64:(h + 1) * 64],
                     Vbw[:, h * 128:(h + 1) * 128], start=True, stop=True,
                     inc=(pe if j == 7 else None))
            evu = p.do("vector", "tensor_copy",
                       usbw[:, grp * 1024:(grp + 1) * 1024], pA[:],
                       waits=[p.ev(pe)], inc=ve)
            free["pA"] = evu
            use("pC", evkb if grp == 0 else None)
            for j in range(8):
                h = grp * 8 + j
                p.do("tensor", "matmul", pC[:, j * 128:(j + 1) * 128],
                     Fw[:, h * 64:(h + 1) * 64],
                     Kbgw[:, h * 128:(h + 1) * 128], start=True, stop=True,
                     inc=(pe if j == 7 else None))
            evwc = p.do("vector", "tensor_copy", wsb[:],
                        pC[:, 0:512], waits=[p.ev(pe), ev_wsb_used], inc=ve)
            use("pK")
            for j in range(4):
                p.do("tensor", "transpose", pK[:, j * 64:(j + 1) * 64],
                     wsb[:, j * 128:(j + 1) * 128], identf[0:64, 0:64],
                     waits=[evwc] if j == 0 else (),
                     inc=(pe if j == 3 else None))
            evt1_ = p.do("vector", "tensor_copy",
                         wtsw[:, grp * 512:grp * 512 + 256], pK[:, 0:256],
                         waits=[p.ev(pe)], inc=ve)
            evwc2 = p.do("vector", "tensor_copy", wsb[:],
                         pC[:, 512:1024], waits=[evt1_], inc=ve)
            free["pC"] = evwc2
            use("pK")
            for j in range(4):
                p.do("tensor", "transpose", pK[:, j * 64:(j + 1) * 64],
                     wsb[:, j * 128:(j + 1) * 128], identf[0:64, 0:64],
                     waits=[evwc2] if j == 0 else (),
                     inc=(pe if j == 3 else None))
            ev_wsb_used = p.do("vector", "tensor_copy",
                               wtsw[:, grp * 512 + 256:(grp + 1) * 512],
                               pK[:, 0:256], waits=[p.ev(pe)], inc=ve)
            free["pK"] = ev_wsb_used
        ev_uw_used = p.ev(ve)
        slots = [(pA, 0), (pA, 512), (pC, 0), (pC, 512)]
        slot_free = [free["pA"], free["pA"], free["pC"], free["pC"]]
        free["pA"] = None
        free["pC"] = None
        ev_vnb = [None] * 8
        ev_kdone = [None] * 8
        ev_glue_prev = d.setdefault("_b2_glue", [None] * 8)
        ev_glue = [None] * 8

        def stage_A(s_):
            ps, off = slots[s_ % 4]
            p.wait("tensor", slot_free[s_ % 4], ev_glue_prev[s_],
                   evz if (ci == 0 and s_ == 0) else None)
            slot_free[s_ % 4] = None
            for j in range(4):
                h = s_ * 4 + j
                p.do("tensor", "matmul", ps[:, off + j * 128: off + (j + 1) * 128],
                     wtsw[:, h * 64:(h + 1) * 64],
                     Sbf[:, h * 128:(h + 1) * 128], start=True, stop=True,
                     inc=(pe if j == 3 else None))
            return p.ev(pe)

        def stage_B(s_, evA):
            ps, off = slots[s_ % 4]
            sp4 = s_ % 4
            evvs = p.do("vector", "tensor_sub", vns[sp4][:],
                        usbw[:, s_ * 512:(s_ + 1) * 512],
                        ps[:, off:off + 512], waits=[evA], inc=ve)
            slot_free[s_ % 4] = evvs
            ev_vnb[s_] = p.do("scalar", "activation", vnb[sp4][:],
                              vns[sp4][:], AF.Copy, waits=[evvs], inc=ac)

        def stage_C(s_):
            h0 = s_ * 4
            use("pB", ev_vnb[s_])
            for j in range(4):
                h = h0 + j
                p.do("tensor", "matmul", pB[:, j * 128:(j + 1) * 128],
                     qf[par][:, (h // 2) * 64:(h // 2 + 1) * 64],
                     Sbf[:, h * 128:(h + 1) * 128], start=True, stop=True)
            for j in range(4):
                h = h0 + j
                p.do("tensor", "matmul",
                     pB[:, 512 + j * 128: 512 + (j + 1) * 128],
                     Aw[:, h * 64:(h + 1) * 64],
                     vnb[s_ % 4][:, j * 128:(j + 1) * 128], start=True,
                     stop=True, inc=(pe if j == 3 else None))
            evqa = p.ev(pe)
            use("pK", evkd if s_ == 0 else None)
            for j in range(4):
                h = h0 + j
                p.do("tensor", "matmul", pK[:, j * 128:(j + 1) * 128],
                     kdww[:, h * 128:(h + 1) * 128],
                     vnb[s_ % 4][:, j * 128:(j + 1) * 128], start=True,
                     stop=True, inc=(pe if j == 3 else None))
            ev_kdone[s_] = p.ev(pe)
            return evqa

        def stage_D(s_, evqa):
            h0 = s_ * 4
            sp2 = s_ % 2

            def sclq():
                return keep["cq_tm"][:, ci * 32 + h0:ci * 32 + h0 + 4].rearrange(
                    "p h -> p h ()").broadcast_to([64, 4, 128])

            evo1 = p.do("vector", "tensor_tensor", v3(osb[sp2], 4),
                        pB[:, 0:512].rearrange("p (h f) -> p h f", h=4),
                        sclq(), ALU.mult,
                        waits=[evqa, ev_glue[s_ - 2] if s_ >= 2 else None],
                        inc=ve)
            evo2 = p.do("vector", "tensor_add",
                        core_st[:, h0 * 128:(h0 + 4) * 128], osb[sp2][:],
                        pB[:, 512:1024],
                        waits=[evo1] + ([ev_st_core] if (s_ == 0 and ev_st_core)
                                        else []), inc=ve)
            free["pB"] = evo2
            egl3 = eglb[:, h0:h0 + 4].rearrange("p h -> p h ()").broadcast_to(
                [128, 4, 128])
            S3 = Ssb[:, h0 * 128:(h0 + 4) * 128].rearrange(
                "p (h dd) -> p h dd", h=4)
            p.do("vector", "tensor_tensor", S3, S3, egl3, ALU.mult,
                 waits=[ev_eglb, ev_kdone[s_]])
            evs2 = p.do("vector", "tensor_add",
                        Ssb[:, h0 * 128:(h0 + 4) * 128],
                        Ssb[:, h0 * 128:(h0 + 4) * 128], pK[:], inc=ve)
            free["pK"] = evs2
            evs3 = p.do("vector", "tensor_copy",
                        Sbf[:, h0 * 128:(h0 + 4) * 128],
                        Ssb[:, h0 * 128:(h0 + 4) * 128], inc=ve)
            ev_glue[s_] = evs3
            ev_glue_prev[s_] = evs3
            return evs3

        evA = {}
        evC = {}
        for s_ in range(8):
            evA[s_] = stage_A(s_)
            if s_ >= 1:
                stage_B(s_ - 1, evA[s_ - 1])
            if s_ >= 2:
                evC[s_ - 2] = stage_C(s_ - 2)
                stage_D(s_ - 2, evC[s_ - 2])
        stage_B(7, evA[7])
        evC[6] = stage_C(6)
        stage_D(6, evC[6])
        evC[7] = stage_C(7)
        stage_D(7, evC[7])
        free["pA"] = [e for e in (slot_free[0], slot_free[1]) if e] or None
        free["pC"] = [e for e in (slot_free[2], slot_free[3]) if e] or None
        ev_eglb_used = p.ev(ve)
        ev_kd_used = p.ev(pe)
        ev_st_core = p.dma("sync", d["core_sp"][csl, :], core_st[:], stc,
                           waits=[p.ev(ve)])
        ev_ld_used["q"][par] = p.ev(pe)
        ev_ld_used["k"][par] = p.ev(pe)
        ev_ld_used["t"][par] = p.ev(ve)
        ev_ld_used["v"] = ev_uw_used
    p.wait("sync", p.ev(stc))
    return


def w4b(t_):
    return t_[:].rearrange("p (k u dd) -> p k u dd", k=2, u=2)


def v3b(ps, n):
    return ps[:, 0:512].rearrange("p (h f) -> p h f", h=n)


def emit_c(p, d, tag="c"):
    nc = p.nc
    pe, ve, ac, po = "pe", "ve", "ac", "po"
    ld5 = p.sem(f"{tag}ld5")
    wz = [p.sem(f"{tag}wz{i}") for i in range(2)]
    stz = [p.sem(f"{tag}stz{i}") for i in range(2)]
    lcr = p.sem(f"{tag}lcr")
    lsz = [p.sem(f"{tag}lsz{i}") for i in range(2)]
    lx1 = [p.sem(f"{tag}lx1{i}") for i in range(2)]
    stx = [p.sem(f"{tag}stx{i}") for i in range(2)]
    wo = p.sem(f"{tag}wo")
    ctx = d["ctx"]
    ident = d["ident"]
    cst = d["cst"]
    sb = lambda name, shape, dt_: ctx.enter_context(nc.sbuf_tensor(name, shape, dt_))
    psum = lambda name, shape, dt_: ctx.enter_context(nc.psum_tensor(name, shape, dt_))

    p1 = ExitStack()
    yfm = p1.enter_context(nc.sbuf_tensor("zyfm", [128, 16 * 2048], BF16))
    wzb = [p1.enter_context(nc.sbuf_tensor(f"zwz{i}", [128, 16 * 512], BF16))
           for i in range(2)]
    szt = [p1.enter_context(nc.sbuf_tensor(f"zszt{i}", [128, 512], BF16))
           for i in range(2)]
    pz = [p1.enter_context(nc.psum_tensor(f"zpz{i}", [128, 512], F32))
          for i in range(2)]
    ey = p.dma("sync", yfm[:], d["y2fm"], ld5)
    ev_wz_used = [None, None]
    ev_sz_used = [None, None]
    ev_pz_free = [None, None]
    for zs in range(8):
        par = zs % 2
        ewz = p.dma("gpsimd", wzb[par][:].rearrange("p (c i) -> p c i", c=16),
                    d["wz"][:, zs * 512:(zs + 1) * 512].rearrange(
                        "(c p) i -> p c i", p=128),
                    wz[par], waits=[ev_wz_used[par]])
        for tt in range(16):
            pzx = pz[tt % 2]
            p.wait("tensor", ewz if tt == 0 else None,
                   ey if (zs == 0 and tt == 0) else None, ev_pz_free[tt % 2])
            ev_pz_free[tt % 2] = None
            for dc in range(16):
                evm = p.do("tensor", "matmul", pzx[:],
                           yfm[:, dc * 2048 + tt * 128: dc * 2048 + (tt + 1) * 128],
                           wzb[par][:, dc * 512:(dc + 1) * 512],
                           start=(dc == 0), stop=(dc == 15),
                           inc=(pe if dc == 15 else None))
            if tt == 15:
                ev_wz_used[par] = p.ev(pe)
            evs = p.do("scalar", "activation", szt[tt % 2][:], pzx[:], AF.Silu,
                       waits=[evm, ev_sz_used[tt % 2]], inc=ac)
            ev_pz_free[tt % 2] = evs
            ev_sz_used[tt % 2] = p.dma(
                "sync", d["sz_sp"][tt * 128:(tt + 1) * 128, zs * 512:(zs + 1) * 512],
                szt[tt % 2][:], stz[tt % 2], waits=[evs])
    p.wait("sync", p.ev(stz[0]), p.ev(stz[1]))
    p.barrier(f"{tag}bar1")
    p1.close()

    wob = sb("cwo", [128, 32 * 2048], BF16)
    crt = sb("ccrt", [128, 4096], BF16)
    szr = [sb(f"cszr{i}", [128, 4096], BF16) for i in range(2)]
    x1t = [sb(f"cx1t{i}", [128, 512], F32) for i in range(2)]
    res = [sb(f"cres{i}", [128, 512], F32) for i in range(2)]
    sq = sb("csq", [128, 4096], BF16)
    rms = sb("crms", [128, 32], F32)
    rstd = sb("crstd", [128, 32], F32)
    ctb = sb("cctb", [128, 4096], BF16)
    cfm = sb("ccfm", [128, 4096], BF16)
    normw = sb("cnw", [128, 1], F32)
    ptr = psum("cptr", [128, 2048], BF16)
    pO = [psum(f"cpO{i}", [128, 512], F32) for i in range(2)]

    env = p.dma("sync", normw[:], d["normw"], ld5)
    ev_wo_ld = []
    for dv in range(32):
        e_ = p.dma("gpsimd", wob[:, dv * 2048:(dv + 1) * 2048],
                   d["wout"][dv * 128:(dv + 1) * 128, :], wo)
        ev_wo_ld.append(e_)
    evnw = []
    for dv in range(32):
        evnw.append(p.do("vector", "tensor_scalar_mul",
                         wob[:, dv * 2048:(dv + 1) * 2048],
                         wob[:, dv * 2048:(dv + 1) * 2048], normw[:],
                         waits=[ev_wo_ld[dv], env], inc=ve))
    ev_crt_used = None
    ev_szr_used = [None, None]
    ev_ptr_ev = None
    ev_pO_free = [None, None]
    ev_x1_used = [None, None]
    ev_res_used = [None, None]
    for tt in range(16):
        par = tt % 2
        ecr = p.dma("sync", crt[:], d["core_sp"][tt * 128:(tt + 1) * 128, :],
                    lcr, waits=[ev_crt_used])
        esz = p.dma("sync", szr[par][:],
                    d["sz_sp"][tt * 128:(tt + 1) * 128, :], lsz[par],
                    waits=[ev_szr_used[par]])
        evsq = p.do("scalar", "activation", sq[:], crt[:], AF.Square,
                    waits=[ecr], inc=ac)
        evrm = p.do("vector", "tensor_reduce",
                    rms[:].rearrange("p h -> p h ()"),
                    sq[:].rearrange("p (h dd) -> p h dd", h=32),
                    AX.X, ALU.add, waits=[evsq], inc=ve)
        evln = p.do("scalar", "activation", rstd[:], rms[:], AF.Ln,
                    scale=float(1.0 / 128.0), bias=cst[:, 1:2], waits=[evrm],
                    inc=ac)
        evrs = p.do("scalar", "activation", rstd[:], rstd[:], AF.Exp, scale=-0.5,
                    waits=[evln], inc=ac)
        evg1 = p.do("vector", "tensor_tensor",
                    ctb[:].rearrange("p (h dd) -> p h dd", h=32),
                    crt[:].rearrange("p (h dd) -> p h dd", h=32),
                    rstd[:].rearrange("p h -> p h ()").broadcast_to([128, 32, 128]),
                    ALU.mult, waits=[evrs], inc=ve)
        ev_crt_used = evg1
        evg2 = p.do("vector", "tensor_mul", ctb[:], ctb[:], szr[par][:],
                    waits=[esz], inc=ve)
        ev_szr_used[par] = evg2
        evcf = None
        for rnd in range(2):
            p.wait("tensor", evg2 if rnd == 0 else None, ev_ptr_ev)
            for j in range(16):
                p.do("tensor", "transpose", ptr[:, j * 128:(j + 1) * 128],
                     ctb[:, rnd * 2048 + j * 128: rnd * 2048 + (j + 1) * 128],
                     ident[:], inc=(pe if j == 15 else None))
            ev_ptr_ev = p.do("vector", "tensor_copy",
                             cfm[:, rnd * 2048:(rnd + 1) * 2048], ptr[:],
                             waits=[p.ev(pe)], inc=ve)
            evcf = ev_ptr_ev
        for ds in range(4):
            pox = pO[ds % 2]
            p.wait("tensor", evcf if ds == 0 else None,
                   evnw[-1] if tt == 0 and ds == 0 else None,
                   ev_pO_free[ds % 2])
            ev_pO_free[ds % 2] = None
            for dv in range(32):
                evm = p.do("tensor", "matmul", pox[:],
                           cfm[:, dv * 128:(dv + 1) * 128],
                           wob[:, dv * 2048 + ds * 512: dv * 2048 + (ds + 1) * 512],
                           start=(dv == 0), stop=(dv == 31),
                           inc=(pe if dv == 31 else None))
            exl = p.dma("sync", x1t[ds % 2][:],
                        d["x1"][tt * 128:(tt + 1) * 128, ds * 512:(ds + 1) * 512],
                        lx1[ds % 2], waits=[ev_x1_used[ds % 2]])
            evr = p.do("vector", "tensor_add", res[ds % 2][:], pox[:],
                       x1t[ds % 2][:],
                       waits=[evm, exl, ev_res_used[ds % 2]], inc=ve)
            ev_pO_free[ds % 2] = evr
            ev_x1_used[ds % 2] = evr
            ev_res_used[ds % 2] = p.dma(
                "sync", d["x2"][tt * 128:(tt + 1) * 128, ds * 512:(ds + 1) * 512],
                res[ds % 2][:], stx[ds % 2], waits=[evr])
    p.wait("sync", p.ev(stx[0]), p.ev(stx[1]))
    return


ALL_STAGES = ("A", "B1", "B2", "C", "D")


def build_program(stages=ALL_STAGES, dbg=()):
    nc = bass.Bass("TRN2", target_bir_lowering=False)
    dram = lambda name, shape, dt_, kind: nc.dram_tensor(name, shape, dt_,
                                                         kind=kind).ap()
    ei = lambda name, shape, dt_: dram(name, shape, dt_, "ExternalInput")

    def scratch(name, shape, dt_):
        kind = "ExternalOutput" if name in dbg else "Internal"
        return dram(name, shape, dt_, kind)

    hs = ei("hs", [S, D], F32)
    ident_h = ei("c_ident", [128, 128], BF16)
    fconst_h = ei("c_fconst", [128, 8], F32)
    out = dram("out", [S, D], F32, "ExternalOutput")
    need_a = "A" in stages
    need_b = "B1" in stages or "B2" in stages
    need_c = "C" in stages
    need_d = "D" in stages
    if need_a or need_d:
        wg_h = ei("ffn_wg", [D, I], F32)
        wu_h = ei("ffn_wu", [D, I], F32)
        wd_h = ei("ffn_wd", [I, D], F32)
        wcache = (scratch("wgc", [16, 128, 8192], BF16),
                  scratch("wuc", [16, 128, 8192], BF16),
                  scratch("wdc", [4, 128, 64 * 512], BF16))
    x1 = scratch("x1", [S, D], F32) if need_a else (
        ei("x1", [S, D], F32) if (need_b or need_c) else None)
    x2 = scratch("x2", [S, D], F32) if need_c else (
        ei("x2", [S, D], F32) if need_d else None)

    d = {}
    if need_b or need_c:
        d["y2fm"] = (scratch("y2fm", [128, 16 * 2048], BF16) if need_b
                     else ei("y2fm", [128, 16 * 2048], BF16))
    if need_b:
        d["wqkv_c"] = ei("wqkv_c", [64, D, 128], F32)
        d["wba"] = ei("wba", [D, 64], F32)
        d["convw"] = ei("convw", [128, 256], F32)
        d["dtb"] = ei("dtb", [128, 32], F32)
        d["alog_rep"] = ei("alog_rep", [128, 32], F32)
        d["colsel"] = ei("colsel", [128, 16 * 32], BF16)
        d["headsel"] = ei("headsel", [32, 2048], F32)
        d["chunksel"] = ei("chunksel", [32, 2048], F32)
        d["identw"] = ei("identw", [64, 2048], BF16)
        d["ones1b"] = ei("ones1b", [1, 128], BF16)
        d["qfm_sp"] = scratch("qfm_sp", [16, 128, S], BF16)
        d["kfm_sp"] = scratch("kfm_sp", [16, 128, S], BF16)
        d["ktm_sp"] = scratch("ktm_sp", [S, 2048], BF16)
        d["vtm_sp"] = scratch("vtm_sp", [S, 4096], BF16)
        d["tb"] = [scratch(f"tb{i}", [32, 2048], F32) for i in range(3)]
        d["glb"] = scratch("glb", [32, 32], F32)
    if need_b or need_c:
        d["core_sp"] = (scratch("core_sp", [S, 4096], BF16) if need_b
                        else ei("core_sp", [S, 4096], BF16))
    if need_c:
        d["wz"] = ei("wz", [D, 4096], F32)
        d["wout"] = ei("wout", [4096, D], F32)
        d["normw"] = ei("normw", [128, 1], F32)
        d["sz_sp"] = scratch("sz_sp", [S, 4096], BF16)
    d["x1"] = x1
    d["x2"] = x2

    p = Prog(nc)
    with p.ctx:
        for s_ in ("pe", "ve", "ac", "po", "ld0", "ld1", "ldg"):
            p.sem(s_)
        gctx = p.ctx.enter_context(ExitStack())
        ident = gctx.enter_context(nc.sbuf_tensor("ident", [128, 128], BF16))
        identf = gctx.enter_context(nc.sbuf_tensor("identf", [128, 128], F32))
        cst = gctx.enter_context(nc.sbuf_tensor("cst", [128, 8], F32))
        eid = p.dma("sync", ident[:], ident_h, "ld0")
        p.dma("gpsimd", identf[:], ident_h, "ldg")
        ecs = p.dma("sync", cst[:], fconst_h, "ld1")
        p.wait("tensor", eid, p.ev("ldg"))
        p.wait("scalar", ecs)
        p.wait("vector", ecs)

        if need_a:
            with ExitStack() as actx:
                t = alloc_ffn_tensors(p, actx, pre='A')
                t["ident"] = ident
                t["cst"] = cst
                dstA = x1 if (need_b or need_c or need_d) else out
                emit_ffn(p, t, hs, dstA, wg_h, wu_h, wd_h, tag="a",
                         cache=wcache, cache_mode="write")
                p.barrier()
        if need_b:
            with ExitStack() as kctx:
                keep = {}
                ksb = lambda name, shape, dt_: kctx.enter_context(
                    nc.sbuf_tensor(name, shape, dt_))
                keep["S_q"] = ksb("kSq", [64, 2048], F32)
                keep["R_q"] = ksb("kRq", [64, 2048], F32)
                keep["R_att"] = ksb("kRa", [64, 2048], F32)
                keep["cq_tm"] = ksb("kcq", [64, 1024], BF16)
                keep["bsg_tm"] = ksb("kbsg", [64, 1024], BF16)
                keep["kdsc_tm"] = ksb("kkd", [64, 1024], BF16)
                keep["beta_tm"] = ksb("kbeta", [64, 1024], F32)
                keep["eglr"] = ksb("keglr", [1, 1024], F32)
                keep["eglrb"] = ksb("keglrb", [1, 1024], BF16)
                d["keep"] = keep
                d["ident"] = ident
                d["identf"] = identf
                d["cst"] = cst
                if "B1" in stages:
                    with ExitStack() as b1ctx:
                        d["ctx"] = b1ctx
                        emit_b1(p, d)
                        p.barrier()
                    if "dbgkeep" in dbg:
                        dk1 = dram("dbg_Sq", [64, 2048], F32, "ExternalOutput")
                        dk2 = dram("dbg_Rq", [64, 2048], F32, "ExternalOutput")
                        dk3 = dram("dbg_Ra", [64, 2048], F32, "ExternalOutput")
                        dk4 = dram("dbg_tm", [64, 1024 * 4], F32, "ExternalOutput")
                        dk5 = dram("dbg_egl", [1, 1024], F32, "ExternalOutput")
                        sdb = p.sem("sdbg")
                        p.dma("sync", dk1, keep["S_q"][:], sdb)
                        p.dma("sync", dk2, keep["R_q"][:], sdb)
                        p.dma("sync", dk3, keep["R_att"][:], sdb)
                        sdbg2 = p.sem("sdbg2")
                        p.dma("gpsimd", dk4.rearrange("p (k x) -> p k x", k=4)[:, 0],
                              keep["cq_tm"][:], sdbg2)
                        p.dma("gpsimd", dk4.rearrange("p (k x) -> p k x", k=4)[:, 1],
                              keep["bsg_tm"][:], sdbg2)
                        p.dma("gpsimd", dk4.rearrange("p (k x) -> p k x", k=4)[:, 2],
                              keep["kdsc_tm"][:], sdbg2)
                        p.dma("sync", dk4.rearrange("p (k x) -> p k x", k=4)[:, 3],
                              keep["beta_tm"][:], sdb)
                        p.wait("sync", p.ev("sdbg2"))
                        p.dma("sync", dk5, keep["eglr"][:], sdb)
                        p.wait("sync", p.ev("sdbg"))
                if "B2" in stages:
                    with ExitStack() as b2ctx:
                        d["ctx"] = b2ctx
                        emit_b2(p, d)
                        p.barrier()
        if need_c:
            with ExitStack() as cctx:
                d["ctx"] = cctx
                d["ident"] = ident
                d["cst"] = cst
                emit_c(p, d)
                p.barrier()
        if need_d:
            with ExitStack() as dctx:
                t = alloc_ffn_tensors(p, dctx, pre='Dx')
                t["ident"] = ident
                t["cst"] = cst
                emit_ffn(p, t, x2, out, wg_h, wu_h, wd_h, tag="d",
                         cache=wcache if need_a else None,
                         cache_mode="read" if need_a else None)
        block = p.ctx.enter_context(nc.Block())
        p.finalize(block)
    return nc


def host_consts():
    ident = np.eye(128, dtype=ml_dtypes.bfloat16)
    colsel = np.zeros((128, 16 * 32), dtype=ml_dtypes.bfloat16)
    for h in range(16):
        colsel[:, h * 32 + 2 * h] = 1
        colsel[:, h * 32 + 2 * h + 1] = 1
    headsel = np.zeros((32, 2048), dtype=np.float32)
    for h in range(32):
        headsel[h, h * 64:(h + 1) * 64] = 1
    chunksel = np.zeros((32, 2048), dtype=np.float32)
    for ci in range(32):
        chunksel[ci, ci * 64:(ci + 1) * 64] = 1
    identw = np.tile(np.eye(64, dtype=ml_dtypes.bfloat16), (1, 32))
    ones1b = np.ones((1, 128), dtype=ml_dtypes.bfloat16)
    return dict(c_ident=ident, c_fconst=FCONST, colsel=colsel, headsel=headsel,
                chunksel=chunksel, identw=identw, ones1b=ones1b)


def host_attn_prep(W_qkv, W_b, W_a, conv_w, dt_bias, A_log):
    wqkv_c = np.ascontiguousarray(
        np.asarray(W_qkv, dtype=np.float32).reshape(D, 64, 128).transpose(1, 0, 2))
    wba = np.ascontiguousarray(
        np.concatenate([np.asarray(W_b, np.float32),
                        np.asarray(W_a, np.float32)], axis=1))
    convw = np.ascontiguousarray(
        np.asarray(conv_w, np.float32).reshape(64, 128, 4).transpose(1, 0, 2)
        .reshape(128, 256))
    dtb = np.ascontiguousarray(np.tile(np.asarray(dt_bias, np.float32), (128, 1)))
    alog_rep = np.ascontiguousarray(np.tile(np.asarray(A_log, np.float32),
                                            (128, 1)))
    return dict(wqkv_c=wqkv_c, wba=wba, convw=convw, dtb=dtb, alog_rep=alog_rep)


def kernel(L0_attn_hidden_states, L0_attn_W_qkv, L0_attn_W_z, L0_attn_W_b,
           L0_attn_W_a, L0_attn_conv_w, L0_attn_dt_bias, L0_attn_A_log,
           L0_attn_norm_w, L0_attn_W_out, L0_ffn_w_gate, L0_ffn_w_up,
           L0_ffn_w_down):
    nc = build_program(stages=ALL_STAGES)
    base = host_consts()
    base.update(host_attn_prep(L0_attn_W_qkv, L0_attn_W_b, L0_attn_W_a,
                               L0_attn_conv_w, L0_attn_dt_bias, L0_attn_A_log))
    base["ffn_wg"] = np.ascontiguousarray(L0_ffn_w_gate, dtype=np.float32)
    base["ffn_wu"] = np.ascontiguousarray(L0_ffn_w_up, dtype=np.float32)
    base["ffn_wd"] = np.ascontiguousarray(L0_ffn_w_down, dtype=np.float32)
    base["wz"] = np.ascontiguousarray(L0_attn_W_z, dtype=np.float32)
    base["wout"] = np.ascontiguousarray(L0_attn_W_out, dtype=np.float32)
    base["normw"] = np.ascontiguousarray(
        np.asarray(L0_attn_norm_w, np.float32).reshape(128, 1))
    in_maps = []
    for c in range(8):
        m = dict(base)
        m["hs"] = np.ascontiguousarray(L0_attn_hidden_states[c], dtype=np.float32)
        in_maps.append(m)
    res = run_bass_kernel_spmd(nc, in_maps, core_ids=list(range(8)))
    return np.stack([r["out"] for r in res.results], axis=0)
